# version 22
# baseline (speedup 1.0000x reference)
import contextlib
import numpy as np
import concourse.bass as bass
import concourse.mybir as mybir
from concourse.bass_utils import run_bass_kernel_spmd

F32 = mybir.dt.float32
BF16 = mybir.dt.bfloat16
AF = mybir.ActivationFunctionType
ALU = mybir.AluOpType
AX = mybir.AxisListType

P = 128
D = 1024
NJ = 8
TOK = 8192
NG = 16
GT = 512
OWN_NT = 16
OWN_NG = 4
NH = 16
DH = 64
HID = 1408
NM = 11
NEX = 9

ENGS = ("sync", "gpsimd", "tensor", "vector", "scalar")
ESEM = {"sync": "sy", "gpsimd": "gp", "tensor": "tn", "vector": "vc", "scalar": "sc"}
DMA_SEMS = ("d_cn", "d_xo0", "d_xo1", "d_xo2", "d_xo3", "d_xs0", "d_xs1",
            "d_wq0", "d_wq1", "d_wq2", "d_wq3", "d_wc", "d_wout", "d_wu", "d_wd", "d_st")


class Sched:

    def __init__(self):
        self.ops = {e: [] for e in ENGS}
        self.inc_pts = {e: [] for e in ENGS}
        self.cnt = {ESEM[e]: 0 for e in ENGS}
        for s in DMA_SEMS:
            self.cnt[s] = 0
        self.buf = {}
        self.waited = {e: {} for e in ENGS}

    def _resolve(self, stamp):
        if stamp[0] == "sem":
            return stamp[1], stamp[2]
        _, e, idx = stamp
        for oi, cum in self.inc_pts[e]:
            if oi >= idx:
                return ESEM[e], cum
        last = len(self.ops[e]) - 1
        assert last >= idx, (e, idx, last)
        while last > 0 and self.ops[e][last].get("is_dma"):
            last -= 1
        if self.ops[e][last].get("is_dma"):
            last = len(self.ops[e])
            self.ops[e].append({"waits": [], "fn": None, "incs": [], "tag": "prog-nop"})
        for oi, cum in self.inc_pts[e]:
            if oi == last:
                return ESEM[e], cum
        self.cnt[ESEM[e]] += 1
        cum = self.cnt[ESEM[e]]
        self.ops[e][last]["incs"].append((ESEM[e], 1))
        self.inc_pts[e].append((last, cum))
        return ESEM[e], cum

    def _bufrec(self, key):
        if key not in self.buf:
            self.buf[key] = {"w": None, "r": []}
        return self.buf[key]

    def _collect_waits(self, eng, reads, writes):
        waits = {}

        def add(stamp):
            if stamp is None:
                return
            if stamp[0] == "pend" and stamp[1] == "tensor" and eng == "tensor":
                return
            s, v = self._resolve(stamp)
            if s == ESEM["tensor"] and eng == "tensor":
                return
            if waits.get(s, 0) < v:
                waits[s] = v

        for k in reads:
            add(self._bufrec(k)["w"])
        for k in writes:
            r = self._bufrec(k)
            add(r["w"])
            for st in r["r"]:
                add(st)
        out = []
        for s, v in waits.items():
            if self.waited[eng].get(s, 0) >= v:
                continue
            self.waited[eng][s] = v
            out.append((s, v))
        return out

    def _stamp_rw(self, stamp, reads, writes):
        for k in reads:
            self._bufrec(k)["r"].append(stamp)
        for k in writes:
            r = self._bufrec(k)
            r["w"] = stamp
            r["r"] = []

    def op(self, eng, fn, reads=(), writes=(), tag=""):
        waits = self._collect_waits(eng, reads, writes)
        idx = len(self.ops[eng])
        self.ops[eng].append({"waits": waits, "fn": fn, "incs": [], "tag": tag})
        self._stamp_rw(("pend", eng, idx), reads, writes)
        return idx

    def dma(self, eng, sem, fn, reads=(), writes=(), tag=""):
        waits = self._collect_waits(eng, reads, writes)
        self.cnt[sem] += 16
        val = self.cnt[sem]
        self.ops[eng].append({"waits": waits, "fn": fn, "incs": [(sem, 16)],
                              "tag": tag, "is_dma": True})
        self._stamp_rw(("sem", sem, val), reads, writes)
        return val

    def wait_manual(self, eng, sem, val):
        if self.waited[eng].get(sem, 0) >= val:
            return
        self.waited[eng][sem] = val
        self.ops[eng].append({"waits": [(sem, val)], "fn": None, "incs": [], "tag": "wait"})

    def stamp_write(self, key, sem, val):
        r = self._bufrec(key)
        r["w"] = ("sem", sem, val)
        r["r"] = []

    def barrier_all(self):
        vals = {}
        for e in ENGS:
            idx = len(self.ops[e]) - 1
            assert idx >= 0
            _, v = self._resolve(("pend", e, idx))
            vals[e] = v
        for e in ENGS:
            for o in ENGS:
                if o != e:
                    self.wait_manual(e, ESEM[o], vals[o])


def build_nc(stage="C", debug=False):
    sub = {"A0": 0, "A1": 1, "A2a": 2, "A2b": 2, "A2c": 2, "A2s": 2, "A2": 2, "A3": 3, "A4": 4, "A5": 5}.get(stage, 9)
    import os as _os
    ng_lim = int(_os.environ.get("NGLIM", "0")) or (2 if stage == "A2s" else NG)
    no_sq = stage in ("A2a", "A2c", "A2s")
    no_evac = stage in ("A2b", "A2c")
    nc = bass.Bass("TRN2", target_bir_lowering=False)

    xs_d = nc.dram_tensor("xs", [TOK, D], F32, kind="ExternalInput").ap()
    wqkv_d = nc.dram_tensor("wqkv", [D, 3072], F32, kind="ExternalInput").ap()
    gamt_d = nc.dram_tensor("gamt", [P, NJ], F32, kind="ExternalInput").ap()
    temp_d = nc.dram_tensor("temp", [1, NH], F32, kind="ExternalInput").ap()
    wout_d = nc.dram_tensor("wout", [D, D], F32, kind="ExternalInput").ap()
    gatew_d = nc.dram_tensor("gatew", [D, 8], F32, kind="ExternalInput").ap()
    ebias_d = nc.dram_tensor("ebias", [1, 8], F32, kind="ExternalInput").ap()
    wup_d = nc.dram_tensor("wup", [NEX, D, HID], F32, kind="ExternalInput").ap()
    wdn_d = nc.dram_tensor("wdn", [NEX, HID, D], F32, kind="ExternalInput").ap()
    idbf_d = nc.dram_tensor("idbf", [P, P], F32, kind="ExternalInput").ap()
    y_d = nc.dram_tensor("y", [2048, D], F32, kind="ExternalOutput").ap()
    if debug:
        dbg = {
            "d_u0": nc.dram_tensor("d_u0", [P, D], F32, kind="ExternalOutput").ap(),
            "d_qk0": nc.dram_tensor("d_qk0", [P, 8192], F32, kind="ExternalOutput").ap(),
            "d_inv": nc.dram_tensor("d_inv", [1, 2048], F32, kind="ExternalOutput").ap(),
            "d_at0": nc.dram_tensor("d_at0", [64, 64], F32, kind="ExternalOutput").ap(),
            "d_vT0": nc.dram_tensor("d_vT0", [P, 2048], F32, kind="ExternalOutput").ap(),
            "d_ao0": nc.dram_tensor("d_ao0", [P, 2048], F32, kind="ExternalOutput").ap(),
            "d_x1": nc.dram_tensor("d_x1", [P, D], F32, kind="ExternalOutput").ap(),
            "d_xn0": nc.dram_tensor("d_xn0", [P, 2048], F32, kind="ExternalOutput").ap(),
            "d_sc": nc.dram_tensor("d_sc", [P, 128], F32, kind="ExternalOutput").ap(),
            "d_s2": nc.dram_tensor("d_s2", [P, 128], F32, kind="ExternalOutput").ap(),
            "d_h0": nc.dram_tensor("d_h0", [P, 512], F32, kind="ExternalOutput").ap(),
            "d_xg0": nc.dram_tensor("d_xg0", [P, D], F32, kind="ExternalOutput").ap(),
        }

    S = Sched()

    with contextlib.ExitStack() as es:
        sbuf = lambda name, shape, dt: es.enter_context(nc.sbuf_tensor(name, shape, dt))
        psum = lambda name, shape, dt: es.enter_context(nc.psum_tensor(name, shape, dt))

        x_own = sbuf("x_own", [P, OWN_NT * D], F32)
        gam_t = sbuf("gam_t", [P, NJ], F32)
        idbf = sbuf("idbf_sb", [P, P], BF16)
        ones_colb = sbuf("ones_colb", [P, 1], BF16)
        ones_rowf = sbuf("ones_rowf", [1, P], F32)
        temp_sb = sbuf("temp_sb", [1, NH], F32)
        et8 = sbuf("et8", [1, NH], F32)
        gate_sb = sbuf("gate_sb", [P, NJ * 8], BF16)
        bias_b = sbuf("bias_b", [P, 8], F32)
        ebias_sb = sbuf("ebias_sb", [1, 8], F32)
        stat = sbuf("stat", [P, 32], F32)
        bnst = sbuf("bnst", [P, 12], F32)
        rq_all = sbuf("rq_all", [64, NH], F32)
        c_all = sbuf("c_all", [64, NH * DH], BF16)

        ps_tr = psum("ps_tr", [P, 1024], BF16)
        ps_A = psum("ps_A", [P, 512], F32)
        ps_B = psum("ps_B", [P, 512], F32)
        ps_G = psum("ps_G", [P, 512], F32)
        ps_C = psum("ps_C", [P, 512], F32)
        ps_R = psum("ps_R", [P, 512], F32)
        ps_Y = psum("ps_Y", [P, 512], F32)
        ps_Z = psum("ps_Z", [P, 512], F32)

        sems = {}
        for nm in list(ESEM.values()) + list(DMA_SEMS):
            sems[nm] = es.enter_context(nc.semaphore(nm))
        S.sems = sems

        S.dma("sync", "d_cn", lambda e: e.dma_start(out=gam_t[:], in_=gamt_d),
              writes=["gam_t"])
        S.dma("sync", "d_cn", lambda e: e.dma_start(out=temp_sb[:], in_=temp_d),
              writes=["temp_sb"])
        S.dma("sync", "d_cn", lambda e: e.dma_start(out=ebias_sb[:], in_=ebias_d),
              writes=["ebias_sb"])
        for k in ("gam_t", "temp_sb", "ebias_sb"):
            S.stamp_write(k, "d_cn", 48)
        S.dma("gpsimd", "d_wc", lambda e: e.dma_start(out=idbf[:], in_=idbf_d),
              writes=["idbf"], tag="idbf")
        S.dma("gpsimd", "d_wc",
              lambda e: e.dma_start(
                  out=gate_sb[:].rearrange("p (j g) -> p j g", j=NJ),
                  in_=gatew_d.rearrange("(j p) g -> p j g", p=P)),
              writes=["gate_sb"], tag="gatew")
        S.stamp_write("idbf", "d_wc", 32)
        S.stamp_write("gate_sb", "d_wc", 32)
        def load_xq(q4):
            S.dma("sync", f"d_xo{q4}",
                  lambda e, q4=q4: e.dma_start(
                      out=x_own[:, q4 * 4 * D:(q4 + 1) * 4 * D]
                          .rearrange("p (i d) -> p i d", d=D),
                      in_=xs_d[q4 * 512:(q4 + 1) * 512, :]
                          .rearrange("(i p) d -> p i d", p=P)),
                  writes=["x_own", f"x_ownq{q4}"], tag=f"x_own{q4}")
        load_xq(0)

        S.op("gpsimd", lambda e: e.memset(ones_colb[:], 1.0), writes=["ones_colb"])
        S.op("gpsimd", lambda e: e.memset(ones_rowf[:], 1.0), writes=["ones_rowf"])

        S.op("scalar", lambda e: e.activation(et8[:], temp_sb[:], AF.Exp),
             reads=["temp_sb"], writes=["et8"])
        S.op("vector", lambda e: e.tensor_scalar_mul(et8[:], et8[:], 8.0),
             reads=["et8"], writes=["et8"])

        S.op("tensor",
             lambda e: e.matmul(ps_Z[0:P, 0:8], ones_rowf[0:1, 0:P], ebias_sb[0:1, :],
                                start=True, stop=True),
             reads=["ones_rowf", "ebias_sb"], writes=["ps_Z"], tag="biasbc")
        S.op("vector", lambda e: e.tensor_copy(bias_b[:], ps_Z[0:P, 0:8]),
             reads=["ps_Z"], writes=["bias_b"])

        esAB = contextlib.ExitStack()
        vT = esAB.enter_context(nc.sbuf_tensor("vT", [P, NJ * 2048], BF16))
        with contextlib.ExitStack() as esA:
            wqkv_sb = esA.enter_context(nc.sbuf_tensor("wqkv_sb", [P, NJ * 3072], BF16))
            x_str = esA.enter_context(nc.sbuf_tensor("x_str", [P, 4 * D], F32))
            u_bf = esA.enter_context(nc.sbuf_tensor("u_bf", [P, 4 * D], BF16))
            uT_g = esA.enter_context(nc.sbuf_tensor("uT_g", [P, 2 * NJ * GT], BF16))
            qk_g = esA.enter_context(nc.sbuf_tensor("qk_g", [P, 4 * 2048], BF16))
            sq_s = esA.enter_context(nc.sbuf_tensor("sq_s", [P, 4 * 512], BF16))

            for q4 in range(4):
                S.dma("gpsimd", f"d_wq{q4}",
                      lambda e, q4=q4: e.dma_start(
                          out=wqkv_sb[:, q4 * 2 * 3072:(q4 + 1) * 2 * 3072]
                              .rearrange("p (j c) -> p j c", j=2),
                          in_=wqkv_d[q4 * 256:(q4 + 1) * 256, :]
                              .rearrange("(j p) c -> p j c", p=P)),
                      writes=["wqkv_sb", f"wqkvq{q4}"], tag=f"wqkv{q4}")
            def emit_folds():
                for q4 in range(4):
                    for j in (2 * q4, 2 * q4 + 1):
                        S.op("scalar",
                             lambda e, j=j: e.mul(
                                 wqkv_sb[:, j * 3072:(j + 1) * 3072],
                                 wqkv_sb[:, j * 3072:(j + 1) * 3072],
                                 gam_t[:, j:j + 1]),
                             reads=[f"wqkvq{q4}", "gam_t"], writes=[f"wqkvf{j}"],
                             tag=f"gammafold{j}")

            def ln_vec(g):
                for s in range(4):
                    if g >= OWN_NG:
                        S.dma("sync", f"d_xs{s % 2}",
                              lambda e, g=g, s=s: e.dma_start(
                                  out=x_str[:, s * D:(s + 1) * D],
                                  in_=xs_d[(g * 4 + s) * P:(g * 4 + s + 1) * P, :]),
                              writes=[f"x_str{s}"], tag=f"xg{g}_{s}")
                for s in range(4):
                    t = g * 4 + s
                    if g < OWN_NG:
                        src = x_own[:, t * D:(t + 1) * D]
                        srckey = f"x_ownq{g}"
                    else:
                        src = x_str[:, s * D:(s + 1) * D]
                        srckey = f"x_str{s}"
                    S.op("vector", lambda e, src=src: e.bn_stats(bnst[:, 0:6], src[:, 0:512]),
                         reads=[srckey], writes=["bnst"])
                    S.op("vector", lambda e, src=src: e.bn_stats(bnst[:, 6:12], src[:, 512:1024]),
                         reads=[srckey], writes=["bnst"])
                    S.op("vector", lambda e, s=s: e.bn_aggr(
                            bass.AP(stat, s * 2, [stat[:].ap[0], [1, 2]]), bnst[:]),
                         reads=["bnst"], writes=["stat"])
                var4 = bass.AP(stat, 1, [stat[:].ap[0], [2, 4]])
                S.op("scalar", lambda e: e.activation(stat[:, 8:12], var4, AF.Sqrt),
                     reads=["stat"], writes=["stat"])
                S.op("vector", lambda e: e.tensor_scalar_max(stat[:, 8:12], stat[:, 8:12],
                                                             1e-20),
                     reads=["stat"], writes=["stat"])
                S.op("vector", lambda e: e.reciprocal(stat[:, 12:16], stat[:, 8:12]),
                     reads=["stat"], writes=["stat"])
                for s in range(4):
                    t = g * 4 + s
                    if g < OWN_NG:
                        src = x_own[:, t * D:(t + 1) * D]
                        srckey = f"x_ownq{g}"
                    else:
                        src = x_str[:, s * D:(s + 1) * D]
                        srckey = f"x_str{s}"
                    ub = u_bf[:, s * D:(s + 1) * D]
                    S.op("vector",
                         lambda e, src=src, ub=ub, s=s: e.tensor_scalar(
                             ub, src, stat[:, 2 * s:2 * s + 1], stat[:, 12 + s:13 + s],
                             ALU.subtract, ALU.mult),
                         reads=[srckey, "stat"], writes=[f"u_bf{s}"], tag=f"ln1_{t}")

            def ln_tensor(g):
                for s in range(4):
                    t = g * 4 + s
                    ub = u_bf[:, s * D:(s + 1) * D]
                    for j in range(NJ):
                        S.op("tensor",
                             lambda e, j=j, ub=ub: e.transpose(ps_tr[:, j * P:(j + 1) * P],
                                                               ub[:, j * P:(j + 1) * P],
                                                               idbf[:]),
                             reads=[f"u_bf{s}", "idbf"], writes=["ps_tr"], tag=f"tr{t}")
                    gb = (g % 2) * NJ * GT
                    dst = uT_g[:, gb + s * P: gb + s * P + 1]
                    dst3 = bass.AP(uT_g, dst.offset, [dst.ap[0], [GT, NJ], [1, P]])
                    src3 = ps_tr[:, 0:NJ * P].rearrange("p (j t) -> p j t", j=NJ)
                    S.op("vector", lambda e, dst3=dst3, src3=src3: e.tensor_copy(dst3, src3),
                         reads=["ps_tr"], writes=[f"uT_g{g % 2}"], tag=f"uTevac{t}")

            glist = list(range(ng_lim)) if sub >= 1 else []
            if glist:
                ln_vec(0)
                ln_tensor(0)
            emit_folds()
            for q4 in range(1, 4):
                load_xq(q4)
            pend_ssq = []
            ep_ctr = [0]

            scr = x_str

            def emit_fin_q():
                S.op("vector", lambda e: e.tensor_copy(scr[0:1, 0:512], ps_R[0:1, 0:512]),
                     reads=["ps_R"], writes=["ssq_scr"])
                S.op("vector", lambda e: e.tensor_copy(scr[0:1, 512:1024], ps_R[32:33, 0:512]),
                     reads=["ps_R"], writes=["ssq_scr"])
                S.op("scalar", lambda e: e.activation(scr[0:1, 0:1024], scr[0:1, 0:1024],
                                                      AF.Sqrt),
                     reads=["ssq_scr"], writes=["ssq_scr"])
                S.op("vector", lambda e: e.tensor_scalar_max(scr[0:1, 0:1024],
                                                             scr[0:1, 0:1024], 1e-12),
                     reads=["ssq_scr"], writes=["ssq_scr"])
                S.op("vector", lambda e: e.reciprocal(scr[0:1, 0:1024], scr[0:1, 0:1024]),
                     reads=["ssq_scr"], writes=["ssq_scr"])

            def emit_fin_k():
                S.op("vector", lambda e: e.tensor_copy(scr[0:1, 1024:1536], ps_Y[0:1, 0:512]),
                     reads=["ps_Y"], writes=["ssq_scrk"])
                S.op("vector", lambda e: e.tensor_copy(scr[0:1, 1536:2048], ps_Y[32:33, 0:512]),
                     reads=["ps_Y"], writes=["ssq_scrk"])
                S.op("scalar", lambda e: e.activation(scr[0:1, 1024:2048],
                                                      scr[0:1, 1024:2048], AF.Sqrt),
                     reads=["ssq_scrk"], writes=["ssq_scrk"])
                S.op("vector", lambda e: e.tensor_scalar_max(scr[0:1, 1024:2048],
                                                             scr[0:1, 1024:2048], 1e-12),
                     reads=["ssq_scrk"], writes=["ssq_scrk"])
                S.op("vector", lambda e: e.reciprocal(scr[0:1, 1024:2048],
                                                      scr[0:1, 1024:2048]),
                     reads=["ssq_scrk"], writes=["ssq_scrk"])
                for h in range(NH):
                    S.op("tensor",
                         lambda e, h=h: e.matmul(ps_R[0:64, h:h + 1],
                                                 scr[0:1, h * DH:(h + 1) * DH],
                                                 et8[0:1, h:h + 1], start=True, stop=True),
                         reads=["ssq_scr", "et8"], writes=["ps_R"], tag=f"rq{h}")
                for h in range(NH):
                    bank, bkey = (ps_Y, "ps_Y") if h < 8 else (ps_Z, "ps_Z")
                    S.op("tensor",
                         lambda e, h=h, bank=bank: e.matmul(
                             bank[0:64, (h % 8) * DH:(h % 8 + 1) * DH],
                             ones_rowf[0:1, 0:64],
                             scr[0:1, 1024 + h * DH: 1024 + (h + 1) * DH],
                             start=True, stop=True),
                         reads=["ssq_scrk", "ones_rowf"], writes=[bkey], tag=f"ct{h}")
                S.op("vector", lambda e: e.tensor_copy(rq_all[:], ps_R[0:64, 0:NH]),
                     reads=["ps_R"], writes=["rq_all"])
                S.op("vector", lambda e: e.tensor_copy(c_all[:, 0:512], ps_Y[0:64, 0:512]),
                     reads=["ps_Y"], writes=["c_all"])
                S.op("vector", lambda e: e.tensor_copy(c_all[:, 512:1024], ps_Z[0:64, 0:512]),
                     reads=["ps_Z"], writes=["c_all"])

            for g in glist:
                if g + 1 < len(glist):
                    ln_vec(g + 1)
                for nch in (range(4) if sub >= 2 else []):
                    for s in range(4):
                        bank, bkey = [(ps_A, "ps_A"), (ps_B, "ps_B"), (ps_C, "ps_C")][(nch * 4 + s) % 3]
                        for j in range(NJ):
                            S.op("tensor",
                                 lambda e, j=j, s=s, nch=nch, bank=bank, g=g: e.matmul(
                                     bank[:],
                                     uT_g[:, (g % 2) * NJ * GT + j * GT + s * P:
                                          (g % 2) * NJ * GT + j * GT + (s + 1) * P],
                                     wqkv_sb[:, j * 3072 + nch * 512: j * 3072 + (nch + 1) * 512],
                                     start=(j == 0), stop=(j == NJ - 1)),
                                 reads=[f"uT_g{g % 2}", f"wqkvf{j}"], writes=[bkey],
                                 tag=f"qk{g}_{nch}_{s}")
                        if not no_evac:
                            S.op("scalar",
                                 lambda e, s=s, nch=nch, bank=bank: e.copy(
                                     qk_g[:, s * 2048 + nch * 512: s * 2048 + (nch + 1) * 512],
                                     bank[:]),
                                 reads=[bkey], writes=["qk_g"], tag=f"qkevac{g}_{nch}_{s}")
                        while sub >= 3 and len(pend_ssq) > (3 if g < NG - 1 else 0):
                            pend_ssq.pop(0)()
                        if sub >= 3 and g == NG - 1 and nch == 2 and s == 0:
                            emit_fin_q()
                        sl = ep_ctr[0] % 4
                        ep_ctr[0] += 1
                        if not no_sq:
                            S.op("scalar",
                                 lambda e, s=s, nch=nch, sl=sl: e.activation(
                                     sq_s[:, sl * 512:(sl + 1) * 512],
                                     qk_g[:, s * 2048 + nch * 512: s * 2048 + (nch + 1) * 512],
                                     AF.Square),
                                 reads=["qk_g"], writes=[f"sq_s{sl}"], tag=f"sq{g}_{nch}_{s}")
                        if sub < 3:
                            continue
                        rbank, rkey = (ps_R, "ps_R") if nch < 2 else (ps_Y, "ps_Y")
                        rb = 32 * (nch % 2)
                        def emit_ssq(rb=rb, g=g, nch=nch, s=s, rbank=rbank, rkey=rkey, sl=sl):
                            S.op("tensor",
                                 lambda e: e.matmul(
                                     rbank[rb:rb + 1, 0:512], ones_colb[:],
                                     sq_s[:, sl * 512:(sl + 1) * 512],
                                     start=(g == 0 and s == 0),
                                     stop=(g == NG - 1 and s == 3),
                                     skip_group_check=True),
                                 reads=[f"sq_s{sl}", "ones_colb"], writes=[rkey],
                                 tag=f"ssq{g}_{nch}_{s}")
                        pend_ssq.append(emit_ssq)

                if debug and g == 0:
                    S.dma("gpsimd", "d_st",
                          lambda e: e.dma_start(out=dbg["d_u0"], in_=u_bf[:, 0:D]),
                          reads=["u_bf0"], tag="dump_u0")
                    S.dma("gpsimd", "d_st",
                          lambda e: e.dma_start(out=dbg["d_qk0"], in_=qk_g[:]),
                          reads=["qk_g"], tag="dump_qk0")
                if g < OWN_NG and sub >= 5:
                    for m in range(NJ):
                        bank, bkey = [(ps_A, "ps_A"), (ps_B, "ps_B"), (ps_C, "ps_C")][m % 3]
                        for j in range(NJ):
                            S.op("tensor",
                                 lambda e, m=m, j=j, bank=bank, g=g: e.matmul(
                                     bank[:],
                                     wqkv_sb[:, j * 3072 + 2048 + m * P:
                                             j * 3072 + 2048 + (m + 1) * P],
                                     uT_g[:, (g % 2) * NJ * GT + j * GT:
                                          (g % 2) * NJ * GT + (j + 1) * GT],
                                     start=(j == 0), stop=(j == NJ - 1)),
                                 reads=[f"uT_g{g % 2}", f"wqkvf{j}"], writes=[bkey],
                                 tag=f"v{g}_{m}")
                        S.op("vector",
                             lambda e, m=m, g=g, bank=bank: e.tensor_copy(
                                 vT[:, m * 2048 + g * GT: m * 2048 + (g + 1) * GT],
                                 bank[:]),
                             reads=[bkey], writes=["vT"], tag=f"vevac{g}_{m}")

                GRAM_MARK = g
                if g + 1 < len(glist):
                    ln_tensor(g + 1)
                for h in (range(NH) if sub >= 4 else []):
                    for s in range(4):
                        hb, hp = 64 * (h % 2), h // 2
                        S.op("tensor",
                             lambda e, h=h, s=s, g=g, hb=hb, hp=hp: e.matmul(
                                 ps_G[hb:hb + 64, hp * DH:(hp + 1) * DH],
                                 qk_g[:, s * 2048 + h * DH: s * 2048 + (h + 1) * DH],
                                 qk_g[:, s * 2048 + 1024 + h * DH: s * 2048 + 1024 + (h + 1) * DH],
                                 start=(g == 0 and s == 0),
                                 stop=(g == NG - 1 and s == 3),
                                 skip_group_check=True),
                             reads=["qk_g"], writes=["ps_G"], tag=f"gram{g}_{h}_{s}")
            for fn_ in pend_ssq:
                fn_()
            pend_ssq = []
            if sub >= 3:
                emit_fin_k()

        S.barrier_all()

        if stage.startswith("A"):
            S.dma("sync", "d_st",
                  lambda e: e.dma_start(
                      out=y_d.rearrange("(i p) d -> p i d", p=P),
                      in_=x_own[:].rearrange("p (i d) -> p i d", d=D)),
                  reads=["x_own"], tag="outA")
            S.wait_manual("sync", "d_st", S.cnt["d_st"])
            esAB.close()
            return _emit(nc, S)

        esCpre = contextlib.ExitStack()
        sc_sb = esCpre.enter_context(nc.sbuf_tensor("sc_sb", [P, OWN_NT * 8], F32, side="right"))
        s2_sb = esCpre.enter_context(nc.sbuf_tensor("s2_sb", [P, OWN_NT * 8], F32, side="right"))
        sm_all = esCpre.enter_context(nc.sbuf_tensor("sm_all", [64, NH * DH], F32, side="right"))
        sme_all = esCpre.enter_context(nc.sbuf_tensor("sme_all", [64, NH * DH], F32, side="right"))
        attn_ba = esCpre.enter_context(nc.sbuf_tensor("attn_ba", [64, NH * DH], BF16, side="right"))
        xn2T = esCpre.enter_context(nc.sbuf_tensor("xn2T", [P, NJ * 2048], BF16, side="right"))
        xn2_bf = esCpre.enter_context(nc.sbuf_tensor("xn2_bf", [P, 4 * D], BF16, side="right"))
        esB = contextlib.ExitStack()
        aoT = esB.enter_context(nc.sbuf_tensor("aoT", [P, NJ * 2048], BF16, side="right"))
        attnM = esB.enter_context(nc.sbuf_tensor("attnM", [P, 8 * P], BF16, side="right"))
        S.op("gpsimd", lambda e: e.memset(attnM[:], 0.0), writes=["attnM"])
        wout_sb = esB.enter_context(nc.sbuf_tensor("wout_sb", [P, NJ * D], BF16, side="right"))
        for q2 in range(2):
            S.dma("gpsimd", "d_wout",
                  lambda e, q2=q2: e.dma_start(
                      out=wout_sb[:, q2 * 4 * D:(q2 + 1) * 4 * D]
                          .rearrange("p (j c) -> p j c", j=4),
                      in_=wout_d[q2 * 512:(q2 + 1) * 512, :]
                          .rearrange("(j p) c -> p j c", p=P)),
                  writes=["wout_sb"], tag=f"wout{q2}")
        S.stamp_write("wout_sb", "d_wout", 32)

        for h in range(NH):
            S.op("vector",
                 lambda e, h=h: e.scalar_tensor_tensor(
                     sm_all[:, h * DH:(h + 1) * DH],
                     ps_G[64 * (h % 2):64 * (h % 2) + 64,
                          (h // 2) * DH:(h // 2 + 1) * DH],
                     rq_all[:, h:h + 1], c_all[:, h * DH:(h + 1) * DH],
                     ALU.mult, ALU.mult),
                 reads=["ps_G", "rq_all", "c_all"], writes=["sm_all"], tag=f"simsc{h}")
        sm3 = sm_all[:].rearrange("p (h e) -> p h e", h=NH)
        sme3 = sme_all[:].rearrange("p (h e) -> p h e", h=NH)
        ab3 = attn_ba[:].rearrange("p (h e) -> p h e", h=NH)
        nmax = stat[0:64, 16:16 + NH]
        nmaxb = bass.AP(stat, 16, [stat[0:64, :].ap[0], [1, NH], [0, DH]])
        S.op("vector", lambda e: e.reduce_max(nmax, sm3, axis=AX.X, negate=True),
             reads=["sm_all"], writes=["stat"])
        S.op("vector", lambda e: e.tensor_tensor(sme3, sm3, nmaxb, ALU.add),
             reads=["sm_all", "stat"], writes=["sme_all"])
        S.op("scalar", lambda e: e.activation(sme_all[:], sme_all[:], AF.Exp),
             reads=["sme_all"], writes=["sme_all"])
        rsum = bass.AP(stat, 0, [stat[0:64, :].ap[0], [1, NH]])
        rsumb = bass.AP(stat, 0, [stat[0:64, :].ap[0], [1, NH], [0, DH]])
        S.op("vector", lambda e: e.reduce_sum(rsum, sme3, axis=AX.X),
             reads=["sme_all"], writes=["stat2"])
        S.op("vector", lambda e: e.reciprocal(rsum, rsum),
             reads=["stat2"], writes=["stat2"])
        S.op("vector", lambda e: e.tensor_tensor(ab3, sme3, rsumb, ALU.mult),
             reads=["sme_all", "stat2"], writes=["attn_ba"])
        for h in range(NH):
            S.op("tensor",
                 lambda e, h=h: e.transpose(ps_tr[0:64, h * DH:(h + 1) * DH],
                                            attn_ba[:, h * DH:(h + 1) * DH],
                                            idbf[0:64, 0:64]),
                 reads=["attn_ba", "idbf"], writes=["ps_tr"], tag=f"attnT{h}")

        def pair_finish(pp):
            S.op("scalar",
                 lambda e, pp=pp: e.copy(attnM[0:64, pp * P: pp * P + 64],
                                         ps_tr[0:64, (2 * pp) * DH:(2 * pp + 1) * DH]),
                 reads=["ps_tr"], writes=["attnM"])
            S.op("scalar",
                 lambda e, pp=pp: e.copy(attnM[64:128, pp * P + 64: (pp + 1) * P],
                                         ps_tr[0:64, (2 * pp + 1) * DH:(2 * pp + 2) * DH]),
                 reads=["ps_tr"], writes=["attnM"])
            for gt in range(4):
                bank, bkey = [(ps_A, "ps_A"), (ps_B, "ps_B"), (ps_C, "ps_C")][(pp * 4 + gt) % 3]
                S.op("tensor",
                     lambda e, pp=pp, gt=gt, bank=bank: e.matmul(
                         bank[:], attnM[:, pp * P:(pp + 1) * P],
                         vT[:, pp * 2048 + gt * GT: pp * 2048 + (gt + 1) * GT],
                         start=True, stop=True),
                     reads=["attnM", "vT"], writes=[bkey], tag=f"av{pp}_{gt}")
                S.op("vector",
                     lambda e, pp=pp, gt=gt, bank=bank: e.tensor_copy(
                         aoT[:, pp * 2048 + gt * GT: pp * 2048 + (gt + 1) * GT], bank[:]),
                     reads=[bkey], writes=["aoT"], tag=f"aoevac{pp}_{gt}")

        for pp in range(8):
            pair_finish(pp)

        def ln2_vec4(g4):
            for s in range(4):
                src_ = x_own[:, (g4 * 4 + s) * D:(g4 * 4 + s + 1) * D]
                S.op("vector", lambda e, src_=src_: e.bn_stats(bnst[:, 0:6], src_[:, 0:512]),
                     reads=["x_own"], writes=["bnst"])
                S.op("vector", lambda e, src_=src_: e.bn_stats(bnst[:, 6:12], src_[:, 512:1024]),
                     reads=["x_own"], writes=["bnst"])
                S.op("vector", lambda e, s=s: e.bn_aggr(
                        bass.AP(stat, s * 2, [stat[:].ap[0], [1, 2]]), bnst[:]),
                     reads=["bnst"], writes=["stat"])
            var4 = bass.AP(stat, 1, [stat[:].ap[0], [2, 4]])
            S.op("vector", lambda e: e.tensor_scalar_add(stat[:, 8:12], var4, 1e-5),
                 reads=["stat"], writes=["stat"])
            S.op("scalar", lambda e: e.activation(stat[:, 8:12], stat[:, 8:12], AF.Sqrt),
                 reads=["stat"], writes=["stat"])
            S.op("vector", lambda e: e.reciprocal(stat[:, 12:16], stat[:, 8:12]),
                 reads=["stat"], writes=["stat"])
            for s in range(4):
                src_ = x_own[:, (g4 * 4 + s) * D:(g4 * 4 + s + 1) * D]
                ub = xn2_bf[:, s * D:(s + 1) * D]
                S.op("vector",
                     lambda e, src_=src_, ub=ub, s=s: e.tensor_scalar(
                         ub, src_, stat[:, 2 * s:2 * s + 1], stat[:, 12 + s:13 + s],
                         ALU.subtract, ALU.mult),
                     reads=["x_own", "stat"], writes=[f"xn2_bf{s}"], tag=f"ln2_{g4 * 4 + s}")

        def ln2_tensor(i):
            ub = xn2_bf[:, (i % 4) * D:((i % 4) + 1) * D]
            ubkey = f"xn2_bf{i % 4}"
            for j in range(NJ):
                S.op("tensor",
                     lambda e, j=j, ub=ub: e.transpose(
                         ps_tr[:, j * P:(j + 1) * P], ub[:, j * P:(j + 1) * P], idbf[:]),
                     reads=[ubkey, "idbf"], writes=["ps_tr"], tag=f"tr2_{i}")
            dst = xn2T[:, i * P: i * P + 1]
            dst3 = bass.AP(xn2T, dst.offset, [dst.ap[0], [2048, NJ], [1, P]])
            src3 = ps_tr[:, 0:NJ * P].rearrange("p (j t) -> p j t", j=NJ)
            S.op("vector", lambda e, dst3=dst3, src3=src3: e.tensor_copy(dst3, src3),
                 reads=["ps_tr"], writes=[f"xn2Tg{i // 4}"], tag=f"xn2Tevac{i}")
            for j in range(NJ):
                S.op("tensor",
                     lambda e, i=i, j=j: e.matmul(
                         ps_R[0:P, 128:136],
                         xn2T[:, j * 2048 + i * P: j * 2048 + (i + 1) * P],
                         gate_sb[:, j * 8:(j + 1) * 8],
                         start=(j == 0), stop=(j == NJ - 1)),
                     reads=[f"xn2Tg{i // 4}", "gate_sb"], writes=["ps_R"], tag=f"gate{i}")
            S.op("scalar",
                 lambda e, i=i: e.activation(sc_sb[:, i * 8:(i + 1) * 8],
                                             ps_R[0:P, 128:136], AF.Sigmoid),
                 reads=["ps_R"], writes=["sc_sb"], tag=f"sig{i}")

        for i in range(OWN_NT):
            for half in range(2):
                bank, bkey = [(ps_A, "ps_A"), (ps_B, "ps_B"), (ps_C, "ps_C")][(i * 2 + half) % 3]
                for j in range(NJ):
                    S.op("tensor",
                         lambda e, i=i, half=half, j=j, bank=bank: e.matmul(
                             bank[:],
                             aoT[:, j * 2048 + (i // 4) * GT + (i % 4) * P:
                                 j * 2048 + (i // 4) * GT + (i % 4 + 1) * P],
                             wout_sb[:, j * D + half * 512: j * D + (half + 1) * 512],
                             start=(j == 0), stop=(j == NJ - 1)),
                         reads=["aoT", "wout_sb"], writes=[bkey], tag=f"wo{i}_{half}")
                S.op("vector",
                     lambda e, i=i, half=half, bank=bank: e.tensor_add(
                         x_own[:, i * D + half * 512: i * D + (half + 1) * 512],
                         x_own[:, i * D + half * 512: i * D + (half + 1) * 512],
                         bank[:]),
                     reads=[bkey, "x_own"], writes=["x_own"], tag=f"resid{i}_{half}")
            if i >= 4:
                ln2_tensor(i - 4)
            if i % 4 == 3:
                ln2_vec4(i // 4)
        for t in range(OWN_NT - 4, OWN_NT):
            ln2_tensor(t)

        if debug:
            S.dma("gpsimd", "d_st", lambda e: e.dma_start(out=dbg["d_vT0"], in_=vT[:, 0:2048]),
                  reads=["vT"], tag="dump_vT0")
            S.dma("gpsimd", "d_st", lambda e: e.dma_start(out=dbg["d_ao0"], in_=aoT[:, 0:2048]),
                  reads=["aoT"], tag="dump_ao0")
            S.dma("gpsimd", "d_st", lambda e: e.dma_start(out=dbg["d_x1"], in_=x_own[:, 0:D]),
                  reads=["x_own"], tag="dump_x1")
        esAB.close()
        esB.close()
        S.barrier_all()

        if stage == "B":
            S.dma("sync", "d_st",
                  lambda e: e.dma_start(
                      out=y_d.rearrange("(i p) d -> p i d", p=P),
                      in_=x_own[:].rearrange("p (i d) -> p i d", d=D)),
                  reads=["x_own"], tag="outB")
            S.wait_manual("sync", "d_st", S.cnt["d_st"])
            return _emit(nc, S)

        with contextlib.ExitStack() as esC:
            wup_sb = esC.enter_context(nc.sbuf_tensor("wup_sb", [P, NJ * HID], BF16))
            wdn_sb = esC.enter_context(nc.sbuf_tensor("wdn_sb", [P, NM * D], BF16))
            h_g = esC.enter_context(nc.sbuf_tensor("h_g", [P, 2 * NM * GT], BF16))

            for i in range(OWN_NT):
                sl = slice(i * 8, (i + 1) * 8)
                S.op("vector",
                     lambda e, sl=sl: e.tensor_add(s2_sb[:, sl], sc_sb[:, sl], bias_b[:]),
                     reads=["sc_sb", "bias_b"], writes=["s2_sb"], tag=f"bias{i}")
                S.op("vector",
                     lambda e, sl=sl: e.reduce_max(stat[:, 16:17], s2_sb[:, sl], axis=AX.X),
                     reads=["s2_sb"], writes=["stat"])
                S.op("vector",
                     lambda e, sl=sl: e.tensor_scalar(stat[:, 18:26], s2_sb[:, sl],
                                                      stat[:, 16:17], -1e30,
                                                      ALU.is_ge, ALU.mult),
                     reads=["s2_sb", "stat"], writes=["stat"], tag=f"eqm{i}")
                S.op("vector",
                     lambda e, sl=sl: e.tensor_add(stat[:, 18:26], stat[:, 18:26],
                                                   s2_sb[:, sl]),
                     reads=["s2_sb", "stat"], writes=["stat"])
                S.op("vector",
                     lambda e: e.reduce_max(stat[:, 17:18], stat[:, 18:26], axis=AX.X),
                     reads=["stat"], writes=["stat"])
                S.op("vector",
                     lambda e, sl=sl: e.tensor_scalar(stat[:, 18:26], s2_sb[:, sl],
                                                      stat[:, 17:18], None, ALU.is_ge),
                     reads=["s2_sb", "stat"], writes=["stat"])
                S.op("vector",
                     lambda e, sl=sl: e.tensor_mul(stat[:, 18:26], stat[:, 18:26],
                                                   sc_sb[:, sl]),
                     reads=["sc_sb", "stat"], writes=["stat"])
                S.op("vector",
                     lambda e, sl=sl: e.tensor_mul(s2_sb[:, sl], stat[:, 18:26],
                                                   stat[:, 18:26]),
                     reads=["stat"], writes=["s2_sb"], tag=f"s2_{i}")

            if debug:
                S.dma("gpsimd", "d_st", lambda e: e.dma_start(out=dbg["d_xn0"], in_=xn2T[:, 0:2048]),
                      reads=["xn2T"], tag="dump_xn0")
                S.dma("gpsimd", "d_st", lambda e: e.dma_start(out=dbg["d_sc"], in_=sc_sb[:]),
                      reads=["sc_sb"], tag="dump_sc")
                S.dma("gpsimd", "d_st", lambda e: e.dma_start(out=dbg["d_s2"], in_=s2_sb[:]),
                      reads=["s2_sb"], tag="dump_s2")
            for gx in range(NEX):
                if gx == 0:
                    for q4_ in range(4):
                        qu = ["d_wu", "d_wd", "d_xs0", "d_xs1"][q4_]
                        S.dma("gpsimd", qu,
                              lambda e, q4_=q4_: e.dma_start(
                                  out=wup_sb[:, q4_ * 2 * HID:(q4_ + 1) * 2 * HID]
                                      .rearrange("p (j f) -> p j f", j=2),
                                  in_=wup_d[0][q4_ * 256:(q4_ + 1) * 256, :]
                                      .rearrange("(j p) f -> p j f", p=P)),
                              writes=["wup_sb", "vT", f"wup0p{q4_}"], tag=f"wup0_{q4_}")
                        S.stamp_write(f"wup0p{q4_}", qu, S.cnt[qu])
                    S.stamp_write("wup_sb", "d_wu", S.cnt["d_wu"])
                else:
                    for q2 in range(2):
                        S.dma("gpsimd", "d_wu",
                              lambda e, gx=gx, q2=q2: e.dma_start(
                                  out=wup_sb[:, q2 * 4 * HID:(q2 + 1) * 4 * HID]
                                      .rearrange("p (j f) -> p j f", j=4),
                                  in_=wup_d[gx][q2 * 512:(q2 + 1) * 512, :]
                                      .rearrange("(j p) f -> p j f", p=P)),
                              writes=["wup_sb"], tag=f"wup{gx}_{q2}")
                    S.stamp_write("wup_sb", "d_wu", S.cnt["d_wu"])
                for q2, (m0, mn) in enumerate(((0, 5), (5, 6))):
                    S.dma("gpsimd", "d_wd",
                          lambda e, gx=gx, m0=m0, mn=mn: e.dma_start(
                              out=wdn_sb[:, m0 * D:(m0 + mn) * D]
                                  .rearrange("p (m c) -> p m c", m=mn),
                              in_=wdn_d[gx][m0 * P:(m0 + mn) * P, :]
                                  .rearrange("(m p) c -> p m c", p=P)),
                          writes=["wdn_sb"] + (["vT"] if gx == 0 else []),
                          tag=f"wdn{gx}_{q2}")
                S.stamp_write("wdn_sb", "d_wd", S.cnt["d_wd"])
                for grp in range(OWN_NG):
                    hb = ((gx * OWN_NG + grp) % 2) * NM * GT
                    hpar = (gx * OWN_NG + grp) % 2
                    for m in range(NM):
                        bank, bkey = [(ps_A, "ps_A"), (ps_B, "ps_B"), (ps_C, "ps_C")][m % 3]
                        for j in range(NJ):
                            S.op("tensor",
                                 lambda e, m=m, j=j, grp=grp, bank=bank: e.matmul(
                                     bank[:],
                                     wup_sb[:, j * HID + m * P: j * HID + (m + 1) * P],
                                     xn2T[:, j * 2048 + grp * GT: j * 2048 + (grp + 1) * GT],
                                     start=(j == 0), stop=(j == NJ - 1)),
                                 reads=(["wup_sb", f"xn2Tg{grp}"] if gx > 0
                                        else ["wup_sb", f"wup0p{j // 2}", f"xn2Tg{grp}"]),
                                 writes=[bkey],
                                 tag=f"up{gx}_{grp}_{m}")
                        S.op("scalar",
                             lambda e, m=m, bank=bank, hb=hb: e.activation(
                                 h_g[:, hb + m * GT: hb + (m + 1) * GT], bank[:], AF.Relu),
                             reads=[bkey], writes=[f"h_g{hpar}_{m}"],
                             tag=f"relu{gx}_{grp}_{m}")
                        S.op("vector",
                             lambda e, m=m, hb=hb: e.tensor_mul(
                                 h_g[:, hb + m * GT: hb + (m + 1) * GT],
                                 h_g[:, hb + m * GT: hb + (m + 1) * GT],
                                 h_g[:, hb + m * GT: hb + (m + 1) * GT]),
                             reads=[f"h_g{hpar}_{m}"], writes=[f"h_g{hpar}_{m}"],
                             tag=f"sqr{gx}_{grp}_{m}")
                        if debug and gx == 0 and grp == 0 and m == 0:
                            S.dma("gpsimd", "d_st",
                                  lambda e: e.dma_start(out=dbg["d_h0"],
                                                        in_=h_g[:, 0:GT]),
                                  reads=["h_g0"], tag="dump_h0")
                    if gx == NEX - 1 and grp > 0:
                        q4 = grp - 1
                        S.dma("sync", "d_st",
                              lambda e, q4=q4: e.dma_start(
                                  out=y_d[q4 * 512:(q4 + 1) * 512, :]
                                      .rearrange("(i p) d -> p i d", p=P),
                                  in_=x_own[:, q4 * 4 * D:(q4 + 1) * 4 * D]
                                      .rearrange("p (i d) -> p i d", d=D)),
                              reads=[f"x_fin{q4}"], tag=f"out{q4}")
                    for s in range(4):
                        i = grp * 4 + s
                        for half in range(2):
                            bank, bkey = (ps_Y, "ps_Y") if (s * 2 + half) % 2 == 0 else (ps_Z, "ps_Z")
                            for m in range(NM):
                                S.op("tensor",
                                     lambda e, m=m, s=s, half=half, bank=bank, hb=hb: e.matmul(
                                         bank[:],
                                         h_g[:, hb + m * GT + s * P: hb + m * GT + (s + 1) * P],
                                         wdn_sb[:, m * D + half * 512:
                                                m * D + (half + 1) * 512],
                                         start=(m == 0), stop=(m == NM - 1)),
                                     reads=[f"h_g{hpar}_{m}", "wdn_sb"], writes=[bkey],
                                     tag=f"dn{gx}_{grp}_{s}_{half}")
                            xsl = x_own[:, i * D + half * 512: i * D + (half + 1) * 512]
                            xkeys = ["x_own"] if gx < 8 else ["x_own", f"x_fin{grp}"]
                            if gx < 8:
                                S.op("vector",
                                     lambda e, xsl=xsl, i=i, gx=gx, bank=bank:
                                     e.scalar_tensor_tensor(
                                         xsl, bank[:],
                                         s2_sb[:, i * 8 + gx: i * 8 + gx + 1], xsl,
                                         ALU.mult, ALU.add),
                                     reads=[bkey, "s2_sb", "x_own"], writes=xkeys,
                                     tag=f"acc{gx}_{i}_{half}")
                            else:
                                S.op("vector",
                                     lambda e, xsl=xsl, bank=bank: e.tensor_add(
                                         xsl, xsl, bank[:]),
                                     reads=[bkey, "x_own"], writes=xkeys,
                                     tag=f"accS_{i}_{half}")

        if debug:
            S.dma("gpsimd", "d_st", lambda e: e.dma_start(out=dbg["d_xg0"],
                                                          in_=x_own[:, 0:D]),
                  reads=["x_own"], tag="dump_xg0")
        esCpre.close()
        S.dma("sync", "d_st",
              lambda e: e.dma_start(
                  out=y_d[3 * 512:4 * 512, :].rearrange("(i p) d -> p i d", p=P),
                  in_=x_own[:, 3 * 4 * D:4 * 4 * D].rearrange("p (i d) -> p i d", d=D)),
              reads=["x_fin3"], tag="out3")
        S.wait_manual("sync", "d_st", S.cnt["d_st"])

        return _emit(nc, S)


def _emit(nc, S):
    sems = S.sems
    with nc.Block() as block:
        def emit(eng_name):
            def body(e):
                for rec in S.ops[eng_name]:
                    for smn, v in rec["waits"]:
                        e.wait_ge(sems[smn], v)
                    inst = rec["fn"](e) if rec["fn"] is not None else None
                    if rec["incs"]:
                        if inst is None:
                            inst = e.nop()
                        for smn, d in rec["incs"]:
                            inst.then_inc(sems[smn], d)
            return body
        block.sync(emit("sync"))
        block.gpsimd(emit("gpsimd"))
        block.tensor(emit("tensor"))
        block.vector(emit("vector"))
        block.scalar(emit("scalar"))
    return nc


_NC = None


def _get_nc():
    global _NC
    if _NC is None:
        _NC = build_nc()
    return _NC


def kernel(L0_attn_x, L0_attn_gamma, L0_attn_w_qkv, L0_attn_temperature, L0_attn_w_out,
           L0_ffn_gate_w, L0_ffn_expert_bias, L0_ffn_w_up, L0_ffn_w_down,
           L0_ffn_shared_w_up, L0_ffn_shared_w_down):
    x = np.ascontiguousarray(np.asarray(L0_attn_x, dtype=np.float32))
    gamma = np.asarray(L0_attn_gamma, dtype=np.float32).reshape(D)
    w_qkv = np.ascontiguousarray(np.asarray(L0_attn_w_qkv, dtype=np.float32))
    temp = np.asarray(L0_attn_temperature, dtype=np.float32).reshape(1, NH)
    w_out = np.ascontiguousarray(np.asarray(L0_attn_w_out, dtype=np.float32))
    gate_w = np.asarray(L0_ffn_gate_w, dtype=np.float32)
    ebias = np.asarray(L0_ffn_expert_bias, dtype=np.float32).reshape(1, 8)
    w_up = np.asarray(L0_ffn_w_up, dtype=np.float32)
    w_dn = np.asarray(L0_ffn_w_down, dtype=np.float32)
    swup = np.asarray(L0_ffn_shared_w_up, dtype=np.float32)
    swdn = np.asarray(L0_ffn_shared_w_down, dtype=np.float32)

    gam_t = np.ascontiguousarray(gamma.reshape(8, P).T)
    gate_wT = np.ascontiguousarray(gate_w.T)
    wupT = np.ascontiguousarray(
        np.concatenate([w_up.transpose(0, 2, 1), swup.T[None]], axis=0))
    wdnT = np.ascontiguousarray(
        np.concatenate([w_dn.transpose(0, 2, 1), swdn.T[None]], axis=0))
    idbf = np.eye(P, dtype=np.float32)

    in_maps = []
    own_slices = []
    for c in range(8):
        b = c // 4
        ownoff = (c % 4) * 2048
        xb = x[b]
        rest = np.r_[0:ownoff, ownoff + 2048:TOK]
        slab = np.ascontiguousarray(
            np.concatenate([xb[ownoff:ownoff + 2048], xb[rest]], axis=0))
        own_slices.append((b, ownoff))
        in_maps.append({
            "xs": slab,
            "wqkv": w_qkv,
            "gamt": gam_t,
            "temp": temp,
            "wout": w_out,
            "gatew": gate_wT,
            "ebias": ebias,
            "wup": wupT,
            "wdn": wdnT,
            "idbf": idbf,
        })

    nc = _get_nc()
    res = run_bass_kernel_spmd(nc, in_maps, core_ids=list(range(8)))

    out = np.empty((2, TOK, D), dtype=np.float32)
    for c in range(8):
        b, ownoff = own_slices[c]
        out[b, ownoff:ownoff + 2048] = res.results[c]["y"]
    return out


# version 23
# speedup vs baseline: 1.0623x; 1.0623x over previous
import contextlib
import numpy as np
import concourse.bass as bass
import concourse.mybir as mybir
from concourse.bass_utils import run_bass_kernel_spmd

F32 = mybir.dt.float32
BF16 = mybir.dt.bfloat16
AF = mybir.ActivationFunctionType
ALU = mybir.AluOpType
AX = mybir.AxisListType

P = 128
D = 1024
NJ = 8
TOK = 8192
NG = 16
GT = 512
OWN_NT = 16
OWN_NG = 4
NH = 16
DH = 64
HID = 1408
NM = 11
NEX = 9

ENGS = ("sync", "gpsimd", "tensor", "vector", "scalar")
ESEM = {"sync": "sy", "gpsimd": "gp", "tensor": "tn", "vector": "vc", "scalar": "sc"}
DMA_SEMS = ("d_cn", "d_xo0", "d_xo1", "d_xo2", "d_xo3", "d_xs0", "d_xs1",
            "d_wq0", "d_wq1", "d_wq2", "d_wq3", "d_wc", "d_wout", "d_wu", "d_wd", "d_st")


class Sched:

    def __init__(self):
        self.ops = {e: [] for e in ENGS}
        self.inc_pts = {e: [] for e in ENGS}
        self.cnt = {ESEM[e]: 0 for e in ENGS}
        for s in DMA_SEMS:
            self.cnt[s] = 0
        self.buf = {}
        self.waited = {e: {} for e in ENGS}

    def _resolve(self, stamp):
        if stamp[0] == "sem":
            return stamp[1], stamp[2]
        _, e, idx = stamp
        for oi, cum in self.inc_pts[e]:
            if oi >= idx:
                return ESEM[e], cum
        last = len(self.ops[e]) - 1
        assert last >= idx, (e, idx, last)
        while last > 0 and self.ops[e][last].get("is_dma"):
            last -= 1
        if self.ops[e][last].get("is_dma"):
            last = len(self.ops[e])
            self.ops[e].append({"waits": [], "fn": None, "incs": [], "tag": "prog-nop"})
        for oi, cum in self.inc_pts[e]:
            if oi == last:
                return ESEM[e], cum
        self.cnt[ESEM[e]] += 1
        cum = self.cnt[ESEM[e]]
        self.ops[e][last]["incs"].append((ESEM[e], 1))
        self.inc_pts[e].append((last, cum))
        return ESEM[e], cum

    def _bufrec(self, key):
        if key not in self.buf:
            self.buf[key] = {"w": None, "r": []}
        return self.buf[key]

    def _collect_waits(self, eng, reads, writes):
        waits = {}

        def add(stamp):
            if stamp is None:
                return
            if stamp[0] == "pend" and stamp[1] == "tensor" and eng == "tensor":
                return
            s, v = self._resolve(stamp)
            if s == ESEM["tensor"] and eng == "tensor":
                return
            if waits.get(s, 0) < v:
                waits[s] = v

        for k in reads:
            add(self._bufrec(k)["w"])
        for k in writes:
            r = self._bufrec(k)
            add(r["w"])
            for st in r["r"]:
                add(st)
        out = []
        for s, v in waits.items():
            if self.waited[eng].get(s, 0) >= v:
                continue
            self.waited[eng][s] = v
            out.append((s, v))
        return out

    def _stamp_rw(self, stamp, reads, writes):
        for k in reads:
            self._bufrec(k)["r"].append(stamp)
        for k in writes:
            r = self._bufrec(k)
            r["w"] = stamp
            r["r"] = []

    def op(self, eng, fn, reads=(), writes=(), tag=""):
        waits = self._collect_waits(eng, reads, writes)
        idx = len(self.ops[eng])
        self.ops[eng].append({"waits": waits, "fn": fn, "incs": [], "tag": tag})
        self._stamp_rw(("pend", eng, idx), reads, writes)
        return idx

    def dma(self, eng, sem, fn, reads=(), writes=(), tag=""):
        waits = self._collect_waits(eng, reads, writes)
        self.cnt[sem] += 16
        val = self.cnt[sem]
        self.ops[eng].append({"waits": waits, "fn": fn, "incs": [(sem, 16)],
                              "tag": tag, "is_dma": True})
        self._stamp_rw(("sem", sem, val), reads, writes)
        return val

    def wait_manual(self, eng, sem, val):
        if self.waited[eng].get(sem, 0) >= val:
            return
        self.waited[eng][sem] = val
        self.ops[eng].append({"waits": [(sem, val)], "fn": None, "incs": [], "tag": "wait"})

    def stamp_write(self, key, sem, val):
        r = self._bufrec(key)
        r["w"] = ("sem", sem, val)
        r["r"] = []

    def barrier_all(self):
        vals = {}
        for e in ENGS:
            idx = len(self.ops[e]) - 1
            assert idx >= 0
            _, v = self._resolve(("pend", e, idx))
            vals[e] = v
        for e in ENGS:
            for o in ENGS:
                if o != e:
                    self.wait_manual(e, ESEM[o], vals[o])


def build_nc(stage="C", debug=False):
    sub = {"A0": 0, "A1": 1, "A2a": 2, "A2b": 2, "A2c": 2, "A2s": 2, "A2": 2, "A3": 3, "A4": 4, "A5": 5}.get(stage, 9)
    import os as _os
    ng_lim = int(_os.environ.get("NGLIM", "0")) or (2 if stage == "A2s" else NG)
    no_sq = stage in ("A2a", "A2c", "A2s")
    no_evac = stage in ("A2b", "A2c")
    nc = bass.Bass("TRN2", target_bir_lowering=False)

    xs_d = nc.dram_tensor("xs", [TOK, D], F32, kind="ExternalInput").ap()
    wqkv_d = nc.dram_tensor("wqkv", [D, 3072], F32, kind="ExternalInput").ap()
    gamt_d = nc.dram_tensor("gamt", [P, NJ], F32, kind="ExternalInput").ap()
    temp_d = nc.dram_tensor("temp", [1, NH], F32, kind="ExternalInput").ap()
    wout_d = nc.dram_tensor("wout", [D, D], F32, kind="ExternalInput").ap()
    gatew_d = nc.dram_tensor("gatew", [D, 8], F32, kind="ExternalInput").ap()
    ebias_d = nc.dram_tensor("ebias", [1, 8], F32, kind="ExternalInput").ap()
    wup_d = nc.dram_tensor("wup", [NEX, D, HID], F32, kind="ExternalInput").ap()
    wdn_d = nc.dram_tensor("wdn", [NEX, HID, D], F32, kind="ExternalInput").ap()
    idbf_d = nc.dram_tensor("idbf", [P, P], F32, kind="ExternalInput").ap()
    y_d = nc.dram_tensor("y", [2048, D], F32, kind="ExternalOutput").ap()
    if debug:
        dbg = {
            "d_u0": nc.dram_tensor("d_u0", [P, D], F32, kind="ExternalOutput").ap(),
            "d_qk0": nc.dram_tensor("d_qk0", [P, 8192], F32, kind="ExternalOutput").ap(),
            "d_inv": nc.dram_tensor("d_inv", [1, 2048], F32, kind="ExternalOutput").ap(),
            "d_at0": nc.dram_tensor("d_at0", [64, 64], F32, kind="ExternalOutput").ap(),
            "d_vT0": nc.dram_tensor("d_vT0", [P, 2048], F32, kind="ExternalOutput").ap(),
            "d_ao0": nc.dram_tensor("d_ao0", [P, 2048], F32, kind="ExternalOutput").ap(),
            "d_x1": nc.dram_tensor("d_x1", [P, D], F32, kind="ExternalOutput").ap(),
            "d_xn0": nc.dram_tensor("d_xn0", [P, 2048], F32, kind="ExternalOutput").ap(),
            "d_sc": nc.dram_tensor("d_sc", [P, 128], F32, kind="ExternalOutput").ap(),
            "d_s2": nc.dram_tensor("d_s2", [P, 128], F32, kind="ExternalOutput").ap(),
            "d_h0": nc.dram_tensor("d_h0", [P, 512], F32, kind="ExternalOutput").ap(),
            "d_xg0": nc.dram_tensor("d_xg0", [P, D], F32, kind="ExternalOutput").ap(),
        }

    S = Sched()

    with contextlib.ExitStack() as es:
        sbuf = lambda name, shape, dt: es.enter_context(nc.sbuf_tensor(name, shape, dt))
        psum = lambda name, shape, dt: es.enter_context(nc.psum_tensor(name, shape, dt))

        x_own = sbuf("x_own", [P, OWN_NT * D], F32)
        gam_t = sbuf("gam_t", [P, NJ], F32)
        idbf = sbuf("idbf_sb", [P, P], BF16)
        ones_colb = sbuf("ones_colb", [P, 1], BF16)
        ones_rowf = sbuf("ones_rowf", [1, P], F32)
        temp_sb = sbuf("temp_sb", [1, NH], F32)
        et8 = sbuf("et8", [1, NH], F32)
        gate_sb = sbuf("gate_sb", [P, NJ * 8], BF16)
        bias_b = sbuf("bias_b", [P, 8], F32)
        ebias_sb = sbuf("ebias_sb", [1, 8], F32)
        stat = sbuf("stat", [P, 32], F32)
        bnst = sbuf("bnst", [P, 12], F32)
        rq_all = sbuf("rq_all", [64, NH], F32)
        c_all = sbuf("c_all", [64, NH * DH], BF16)

        ps_tr = psum("ps_tr", [P, 1024], BF16)
        ps_A = psum("ps_A", [P, 512], F32)
        ps_B = psum("ps_B", [P, 512], F32)
        ps_G = psum("ps_G", [P, 512], F32)
        ps_C = psum("ps_C", [P, 512], F32)
        ps_R = psum("ps_R", [P, 512], F32)
        ps_Y = psum("ps_Y", [P, 512], F32)
        ps_Z = psum("ps_Z", [P, 512], F32)

        sems = {}
        for nm in list(ESEM.values()) + list(DMA_SEMS):
            sems[nm] = es.enter_context(nc.semaphore(nm))
        S.sems = sems

        S.dma("sync", "d_cn", lambda e: e.dma_start(out=gam_t[:], in_=gamt_d),
              writes=["gam_t"])
        S.dma("sync", "d_cn", lambda e: e.dma_start(out=temp_sb[:], in_=temp_d),
              writes=["temp_sb"])
        S.dma("sync", "d_cn", lambda e: e.dma_start(out=ebias_sb[:], in_=ebias_d),
              writes=["ebias_sb"])
        for k in ("gam_t", "temp_sb", "ebias_sb"):
            S.stamp_write(k, "d_cn", 48)
        S.dma("gpsimd", "d_wc", lambda e: e.dma_start(out=idbf[:], in_=idbf_d),
              writes=["idbf"], tag="idbf")
        S.dma("gpsimd", "d_wc",
              lambda e: e.dma_start(
                  out=gate_sb[:].rearrange("p (j g) -> p j g", j=NJ),
                  in_=gatew_d.rearrange("(j p) g -> p j g", p=P)),
              writes=["gate_sb"], tag="gatew")
        S.stamp_write("idbf", "d_wc", 32)
        S.stamp_write("gate_sb", "d_wc", 32)
        def load_xq(q4):
            S.dma("sync", f"d_xo{q4}",
                  lambda e, q4=q4: e.dma_start(
                      out=x_own[:, q4 * 4 * D:(q4 + 1) * 4 * D]
                          .rearrange("p (i d) -> p i d", d=D),
                      in_=xs_d[q4 * 512:(q4 + 1) * 512, :]
                          .rearrange("(i p) d -> p i d", p=P)),
                  writes=["x_own", f"x_ownq{q4}"], tag=f"x_own{q4}")
        load_xq(0)

        S.op("gpsimd", lambda e: e.memset(ones_colb[:], 1.0), writes=["ones_colb"])
        S.op("gpsimd", lambda e: e.memset(ones_rowf[:], 1.0), writes=["ones_rowf"])

        S.op("scalar", lambda e: e.activation(et8[:], temp_sb[:], AF.Exp),
             reads=["temp_sb"], writes=["et8"])
        S.op("vector", lambda e: e.tensor_scalar_mul(et8[:], et8[:], 8.0),
             reads=["et8"], writes=["et8"])

        S.op("tensor",
             lambda e: e.matmul(ps_Z[0:P, 0:8], ones_rowf[0:1, 0:P], ebias_sb[0:1, :],
                                start=True, stop=True),
             reads=["ones_rowf", "ebias_sb"], writes=["ps_Z"], tag="biasbc")
        S.op("vector", lambda e: e.tensor_copy(bias_b[:], ps_Z[0:P, 0:8]),
             reads=["ps_Z"], writes=["bias_b"])

        esAB = contextlib.ExitStack()
        vT = esAB.enter_context(nc.sbuf_tensor("vT", [P, NJ * 2048], BF16))
        with contextlib.ExitStack() as esA:
            wqkv_sb = esA.enter_context(nc.sbuf_tensor("wqkv_sb", [P, NJ * 3072], BF16))
            x_str = esA.enter_context(nc.sbuf_tensor("x_str", [P, 4 * D], F32))
            u_bf = esA.enter_context(nc.sbuf_tensor("u_bf", [P, 4 * D], BF16))
            uT_g = esA.enter_context(nc.sbuf_tensor("uT_g", [P, 2 * NJ * GT], BF16))
            qk_g = esA.enter_context(nc.sbuf_tensor("qk_g", [P, 4 * 2048], BF16))
            sq_s = esA.enter_context(nc.sbuf_tensor("sq_s", [P, 4 * 512], BF16))

            for q4 in range(4):
                S.dma("gpsimd", f"d_wq{q4}",
                      lambda e, q4=q4: e.dma_start(
                          out=wqkv_sb[:, q4 * 2 * 3072:(q4 + 1) * 2 * 3072]
                              .rearrange("p (j c) -> p j c", j=2),
                          in_=wqkv_d[q4 * 256:(q4 + 1) * 256, :]
                              .rearrange("(j p) c -> p j c", p=P)),
                      writes=["wqkv_sb", f"wqkvq{q4}"], tag=f"wqkv{q4}")
            def emit_folds():
                for q4 in range(4):
                    for j in (2 * q4, 2 * q4 + 1):
                        S.op("scalar",
                             lambda e, j=j: e.mul(
                                 wqkv_sb[:, j * 3072:(j + 1) * 3072],
                                 wqkv_sb[:, j * 3072:(j + 1) * 3072],
                                 gam_t[:, j:j + 1]),
                             reads=[f"wqkvq{q4}", "gam_t"], writes=[f"wqkvf{j}"],
                             tag=f"gammafold{j}")

            def ln_vec(g):
                for s in range(4):
                    if g >= OWN_NG:
                        S.dma("sync", f"d_xs{s % 2}",
                              lambda e, g=g, s=s: e.dma_start(
                                  out=x_str[:, s * D:(s + 1) * D],
                                  in_=xs_d[(g * 4 + s) * P:(g * 4 + s + 1) * P, :]),
                              writes=[f"x_str{s}"], tag=f"xg{g}_{s}")
                for s in range(4):
                    t = g * 4 + s
                    if g < OWN_NG:
                        src = x_own[:, t * D:(t + 1) * D]
                        srckey = f"x_ownq{g}"
                    else:
                        src = x_str[:, s * D:(s + 1) * D]
                        srckey = f"x_str{s}"
                    S.op("vector", lambda e, src=src: e.bn_stats(bnst[:, 0:6], src[:, 0:512]),
                         reads=[srckey], writes=["bnst"])
                    S.op("vector", lambda e, src=src: e.bn_stats(bnst[:, 6:12], src[:, 512:1024]),
                         reads=[srckey], writes=["bnst"])
                    S.op("vector", lambda e, s=s: e.bn_aggr(
                            bass.AP(stat, s * 2, [stat[:].ap[0], [1, 2]]), bnst[:]),
                         reads=["bnst"], writes=["stat"])
                var4 = bass.AP(stat, 1, [stat[:].ap[0], [2, 4]])
                S.op("scalar", lambda e: e.activation(stat[:, 8:12], var4, AF.Sqrt),
                     reads=["stat"], writes=["stat"])
                S.op("vector", lambda e: e.tensor_scalar_max(stat[:, 8:12], stat[:, 8:12],
                                                             1e-20),
                     reads=["stat"], writes=["stat"])
                S.op("vector", lambda e: e.reciprocal(stat[:, 12:16], stat[:, 8:12]),
                     reads=["stat"], writes=["stat"])
                for s in range(4):
                    t = g * 4 + s
                    if g < OWN_NG:
                        src = x_own[:, t * D:(t + 1) * D]
                        srckey = f"x_ownq{g}"
                    else:
                        src = x_str[:, s * D:(s + 1) * D]
                        srckey = f"x_str{s}"
                    ub = u_bf[:, s * D:(s + 1) * D]
                    S.op("vector",
                         lambda e, src=src, ub=ub, s=s: e.tensor_scalar(
                             ub, src, stat[:, 2 * s:2 * s + 1], stat[:, 12 + s:13 + s],
                             ALU.subtract, ALU.mult),
                         reads=[srckey, "stat"], writes=[f"u_bf{s}"], tag=f"ln1_{t}")

            def ln_tensor(g):
                for s in range(4):
                    t = g * 4 + s
                    ub = u_bf[:, s * D:(s + 1) * D]
                    for j in range(NJ):
                        S.op("tensor",
                             lambda e, j=j, ub=ub: e.transpose(ps_tr[:, j * P:(j + 1) * P],
                                                               ub[:, j * P:(j + 1) * P],
                                                               idbf[:]),
                             reads=[f"u_bf{s}", "idbf"], writes=["ps_tr"], tag=f"tr{t}")
                    gb = (g % 2) * NJ * GT
                    dst = uT_g[:, gb + s * P: gb + s * P + 1]
                    dst3 = bass.AP(uT_g, dst.offset, [dst.ap[0], [GT, NJ], [1, P]])
                    src3 = ps_tr[:, 0:NJ * P].rearrange("p (j t) -> p j t", j=NJ)
                    S.op("vector", lambda e, dst3=dst3, src3=src3: e.tensor_copy(dst3, src3),
                         reads=["ps_tr"], writes=[f"uT_g{g % 2}"], tag=f"uTevac{t}")

            glist = list(range(ng_lim)) if sub >= 1 else []
            if glist:
                ln_vec(0)
                ln_tensor(0)
            emit_folds()
            for q4 in range(1, 4):
                load_xq(q4)
            pend_ssq = []
            ep_ctr = [0]
            for g in glist:
                if g + 1 < len(glist):
                    ln_vec(g + 1)
                for nch in (range(4) if sub >= 2 else []):
                    for s in range(4):
                        bank, bkey = [(ps_A, "ps_A"), (ps_B, "ps_B"), (ps_C, "ps_C")][(nch * 4 + s) % 3]
                        for j in range(NJ):
                            S.op("tensor",
                                 lambda e, j=j, s=s, nch=nch, bank=bank, g=g: e.matmul(
                                     bank[:],
                                     uT_g[:, (g % 2) * NJ * GT + j * GT + s * P:
                                          (g % 2) * NJ * GT + j * GT + (s + 1) * P],
                                     wqkv_sb[:, j * 3072 + nch * 512: j * 3072 + (nch + 1) * 512],
                                     start=(j == 0), stop=(j == NJ - 1)),
                                 reads=[f"uT_g{g % 2}", f"wqkvf{j}"], writes=[bkey],
                                 tag=f"qk{g}_{nch}_{s}")
                        if not no_evac:
                            S.op("scalar",
                                 lambda e, s=s, nch=nch, bank=bank: e.copy(
                                     qk_g[:, s * 2048 + nch * 512: s * 2048 + (nch + 1) * 512],
                                     bank[:]),
                                 reads=[bkey], writes=["qk_g"], tag=f"qkevac{g}_{nch}_{s}")
                        if sub >= 3 and len(pend_ssq) > 3:
                            pend_ssq.pop(0)()
                        sl = ep_ctr[0] % 4
                        ep_ctr[0] += 1
                        if not no_sq:
                            S.op("scalar",
                                 lambda e, s=s, nch=nch, sl=sl: e.activation(
                                     sq_s[:, sl * 512:(sl + 1) * 512],
                                     qk_g[:, s * 2048 + nch * 512: s * 2048 + (nch + 1) * 512],
                                     AF.Square),
                                 reads=["qk_g"], writes=[f"sq_s{sl}"], tag=f"sq{g}_{nch}_{s}")
                        if sub < 3:
                            continue
                        rbank, rkey = (ps_R, "ps_R") if nch < 2 else (ps_Y, "ps_Y")
                        rb = 32 * (nch % 2)
                        def emit_ssq(rb=rb, g=g, nch=nch, s=s, rbank=rbank, rkey=rkey, sl=sl):
                            S.op("tensor",
                                 lambda e: e.matmul(
                                     rbank[rb:rb + 1, 0:512], ones_colb[:],
                                     sq_s[:, sl * 512:(sl + 1) * 512],
                                     start=(g == 0 and s == 0),
                                     stop=(g == NG - 1 and s == 3),
                                     skip_group_check=True),
                                 reads=[f"sq_s{sl}", "ones_colb"], writes=[rkey],
                                 tag=f"ssq{g}_{nch}_{s}")
                        pend_ssq.append(emit_ssq)

                if debug and g == 0:
                    S.dma("gpsimd", "d_st",
                          lambda e: e.dma_start(out=dbg["d_u0"], in_=u_bf[:, 0:D]),
                          reads=["u_bf0"], tag="dump_u0")
                    S.dma("gpsimd", "d_st",
                          lambda e: e.dma_start(out=dbg["d_qk0"], in_=qk_g[:]),
                          reads=["qk_g"], tag="dump_qk0")
                if g < OWN_NG and sub >= 5:
                    for m in range(NJ):
                        bank, bkey = [(ps_A, "ps_A"), (ps_B, "ps_B"), (ps_C, "ps_C")][m % 3]
                        for j in range(NJ):
                            S.op("tensor",
                                 lambda e, m=m, j=j, bank=bank, g=g: e.matmul(
                                     bank[:],
                                     wqkv_sb[:, j * 3072 + 2048 + m * P:
                                             j * 3072 + 2048 + (m + 1) * P],
                                     uT_g[:, (g % 2) * NJ * GT + j * GT:
                                          (g % 2) * NJ * GT + (j + 1) * GT],
                                     start=(j == 0), stop=(j == NJ - 1)),
                                 reads=[f"uT_g{g % 2}", f"wqkvf{j}"], writes=[bkey],
                                 tag=f"v{g}_{m}")
                        S.op("vector",
                             lambda e, m=m, g=g, bank=bank: e.tensor_copy(
                                 vT[:, m * 2048 + g * GT: m * 2048 + (g + 1) * GT],
                                 bank[:]),
                             reads=[bkey], writes=["vT"], tag=f"vevac{g}_{m}")

                GRAM_MARK = g
                if g + 1 < len(glist):
                    ln_tensor(g + 1)
                for h in (range(NH) if sub >= 4 else []):
                    for s in range(4):
                        hb, hp = 64 * (h % 2), h // 2
                        S.op("tensor",
                             lambda e, h=h, s=s, g=g, hb=hb, hp=hp: e.matmul(
                                 ps_G[hb:hb + 64, hp * DH:(hp + 1) * DH],
                                 qk_g[:, s * 2048 + h * DH: s * 2048 + (h + 1) * DH],
                                 qk_g[:, s * 2048 + 1024 + h * DH: s * 2048 + 1024 + (h + 1) * DH],
                                 start=(g == 0 and s == 0),
                                 stop=(g == NG - 1 and s == 3),
                                 skip_group_check=True),
                             reads=["qk_g"], writes=["ps_G"], tag=f"gram{g}_{h}_{s}")
            for fn_ in pend_ssq:
                fn_()
            pend_ssq = []

            if sub >= 3:
                scr = x_str
                S.op("vector", lambda e: e.tensor_copy(scr[0:1, 0:512], ps_R[0:1, 0:512]),
                     reads=["ps_R"], writes=["ssq_scr"])
                S.op("vector", lambda e: e.tensor_copy(scr[0:1, 512:1024], ps_R[32:33, 0:512]),
                     reads=["ps_R"], writes=["ssq_scr"])
                S.op("scalar", lambda e: e.activation(scr[0:1, 0:1024], scr[0:1, 0:1024],
                                                      AF.Sqrt),
                     reads=["ssq_scr"], writes=["ssq_scr"])
                S.op("vector", lambda e: e.tensor_scalar_max(scr[0:1, 0:1024],
                                                             scr[0:1, 0:1024], 1e-12),
                     reads=["ssq_scr"], writes=["ssq_scr"])
                S.op("vector", lambda e: e.reciprocal(scr[0:1, 0:1024], scr[0:1, 0:1024]),
                     reads=["ssq_scr"], writes=["ssq_scr"])
                S.op("vector", lambda e: e.tensor_copy(scr[0:1, 1024:1536], ps_Y[0:1, 0:512]),
                     reads=["ps_Y"], writes=["ssq_scrk"])
                S.op("vector", lambda e: e.tensor_copy(scr[0:1, 1536:2048], ps_Y[32:33, 0:512]),
                     reads=["ps_Y"], writes=["ssq_scrk"])
                S.op("scalar", lambda e: e.activation(scr[0:1, 1024:2048],
                                                      scr[0:1, 1024:2048], AF.Sqrt),
                     reads=["ssq_scrk"], writes=["ssq_scrk"])
                S.op("vector", lambda e: e.tensor_scalar_max(scr[0:1, 1024:2048],
                                                             scr[0:1, 1024:2048], 1e-12),
                     reads=["ssq_scrk"], writes=["ssq_scrk"])
                S.op("vector", lambda e: e.reciprocal(scr[0:1, 1024:2048],
                                                      scr[0:1, 1024:2048]),
                     reads=["ssq_scrk"], writes=["ssq_scrk"])
                for h in range(NH):
                    S.op("tensor",
                         lambda e, h=h: e.matmul(ps_R[0:64, h:h + 1],
                                                 scr[0:1, h * DH:(h + 1) * DH],
                                                 et8[0:1, h:h + 1], start=True, stop=True),
                         reads=["ssq_scr", "et8"], writes=["ps_R"], tag=f"rq{h}")
                for h in range(NH):
                    bank, bkey = (ps_Y, "ps_Y") if h < 8 else (ps_Z, "ps_Z")
                    S.op("tensor",
                         lambda e, h=h, bank=bank: e.matmul(
                             bank[0:64, (h % 8) * DH:(h % 8 + 1) * DH],
                             ones_rowf[0:1, 0:64],
                             scr[0:1, 1024 + h * DH: 1024 + (h + 1) * DH],
                             start=True, stop=True),
                         reads=["ssq_scrk", "ones_rowf"], writes=[bkey], tag=f"ct{h}")
                S.op("vector", lambda e: e.tensor_copy(rq_all[:], ps_R[0:64, 0:NH]),
                     reads=["ps_R"], writes=["rq_all"])
                S.op("vector", lambda e: e.tensor_copy(c_all[:, 0:512], ps_Y[0:64, 0:512]),
                     reads=["ps_Y"], writes=["c_all"])
                S.op("vector", lambda e: e.tensor_copy(c_all[:, 512:1024], ps_Z[0:64, 0:512]),
                     reads=["ps_Z"], writes=["c_all"])

        S.barrier_all()

        if stage.startswith("A"):
            S.dma("sync", "d_st",
                  lambda e: e.dma_start(
                      out=y_d.rearrange("(i p) d -> p i d", p=P),
                      in_=x_own[:].rearrange("p (i d) -> p i d", d=D)),
                  reads=["x_own"], tag="outA")
            S.wait_manual("sync", "d_st", S.cnt["d_st"])
            esAB.close()
            return _emit(nc, S)

        esCpre = contextlib.ExitStack()
        xn2_tmp = esCpre.enter_context(nc.sbuf_tensor("xn2_tmp", [P, D], F32, side="right"))
        sc_sb = esCpre.enter_context(nc.sbuf_tensor("sc_sb", [P, OWN_NT * 8], F32, side="right"))
        s2_sb = esCpre.enter_context(nc.sbuf_tensor("s2_sb", [P, OWN_NT * 8], F32, side="right"))
        sm_all = esCpre.enter_context(nc.sbuf_tensor("sm_all", [64, NH * DH], F32, side="right"))
        sme_all = esCpre.enter_context(nc.sbuf_tensor("sme_all", [64, NH * DH], F32, side="right"))
        attn_ba = esCpre.enter_context(nc.sbuf_tensor("attn_ba", [64, NH * DH], BF16, side="right"))
        xn2T = esCpre.enter_context(nc.sbuf_tensor("xn2T", [P, NJ * 2048], BF16, side="right"))
        xn2_bf = esCpre.enter_context(nc.sbuf_tensor("xn2_bf", [P, 2 * D], BF16, side="right"))
        esB = contextlib.ExitStack()
        aoT = esB.enter_context(nc.sbuf_tensor("aoT", [P, NJ * 2048], BF16, side="right"))
        attnM = esB.enter_context(nc.sbuf_tensor("attnM", [P, 8 * P], BF16, side="right"))
        S.op("gpsimd", lambda e: e.memset(attnM[:], 0.0), writes=["attnM"])
        wout_sb = esB.enter_context(nc.sbuf_tensor("wout_sb", [P, NJ * D], BF16, side="right"))
        for q2 in range(2):
            S.dma("gpsimd", "d_wout",
                  lambda e, q2=q2: e.dma_start(
                      out=wout_sb[:, q2 * 4 * D:(q2 + 1) * 4 * D]
                          .rearrange("p (j c) -> p j c", j=4),
                      in_=wout_d[q2 * 512:(q2 + 1) * 512, :]
                          .rearrange("(j p) c -> p j c", p=P)),
                  writes=["wout_sb"], tag=f"wout{q2}")
        S.stamp_write("wout_sb", "d_wout", 32)

        for h in range(NH):
            S.op("vector",
                 lambda e, h=h: e.scalar_tensor_tensor(
                     sm_all[:, h * DH:(h + 1) * DH],
                     ps_G[64 * (h % 2):64 * (h % 2) + 64,
                          (h // 2) * DH:(h // 2 + 1) * DH],
                     rq_all[:, h:h + 1], c_all[:, h * DH:(h + 1) * DH],
                     ALU.mult, ALU.mult),
                 reads=["ps_G", "rq_all", "c_all"], writes=["sm_all"], tag=f"simsc{h}")
        sm3 = sm_all[:].rearrange("p (h e) -> p h e", h=NH)
        sme3 = sme_all[:].rearrange("p (h e) -> p h e", h=NH)
        ab3 = attn_ba[:].rearrange("p (h e) -> p h e", h=NH)
        nmax = stat[0:64, 16:16 + NH]
        nmaxb = bass.AP(stat, 16, [stat[0:64, :].ap[0], [1, NH], [0, DH]])
        S.op("vector", lambda e: e.reduce_max(nmax, sm3, axis=AX.X, negate=True),
             reads=["sm_all"], writes=["stat"])
        S.op("vector", lambda e: e.tensor_tensor(sme3, sm3, nmaxb, ALU.add),
             reads=["sm_all", "stat"], writes=["sme_all"])
        S.op("scalar", lambda e: e.activation(sme_all[:], sme_all[:], AF.Exp),
             reads=["sme_all"], writes=["sme_all"])
        rsum = bass.AP(stat, 0, [stat[0:64, :].ap[0], [1, NH]])
        rsumb = bass.AP(stat, 0, [stat[0:64, :].ap[0], [1, NH], [0, DH]])
        S.op("vector", lambda e: e.reduce_sum(rsum, sme3, axis=AX.X),
             reads=["sme_all"], writes=["stat2"])
        S.op("vector", lambda e: e.reciprocal(rsum, rsum),
             reads=["stat2"], writes=["stat2"])
        S.op("vector", lambda e: e.tensor_tensor(ab3, sme3, rsumb, ALU.mult),
             reads=["sme_all", "stat2"], writes=["attn_ba"])
        for h in range(NH):
            S.op("tensor",
                 lambda e, h=h: e.transpose(ps_tr[0:64, h * DH:(h + 1) * DH],
                                            attn_ba[:, h * DH:(h + 1) * DH],
                                            idbf[0:64, 0:64]),
                 reads=["attn_ba", "idbf"], writes=["ps_tr"], tag=f"attnT{h}")

        def pair_finish(pp):
            S.op("scalar",
                 lambda e, pp=pp: e.copy(attnM[0:64, pp * P: pp * P + 64],
                                         ps_tr[0:64, (2 * pp) * DH:(2 * pp + 1) * DH]),
                 reads=["ps_tr"], writes=["attnM"])
            S.op("scalar",
                 lambda e, pp=pp: e.copy(attnM[64:128, pp * P + 64: (pp + 1) * P],
                                         ps_tr[0:64, (2 * pp + 1) * DH:(2 * pp + 2) * DH]),
                 reads=["ps_tr"], writes=["attnM"])
            for gt in range(4):
                bank, bkey = [(ps_A, "ps_A"), (ps_B, "ps_B"), (ps_C, "ps_C")][(pp * 4 + gt) % 3]
                S.op("tensor",
                     lambda e, pp=pp, gt=gt, bank=bank: e.matmul(
                         bank[:], attnM[:, pp * P:(pp + 1) * P],
                         vT[:, pp * 2048 + gt * GT: pp * 2048 + (gt + 1) * GT],
                         start=True, stop=True),
                     reads=["attnM", "vT"], writes=[bkey], tag=f"av{pp}_{gt}")
                S.op("vector",
                     lambda e, pp=pp, gt=gt, bank=bank: e.tensor_copy(
                         aoT[:, pp * 2048 + gt * GT: pp * 2048 + (gt + 1) * GT], bank[:]),
                     reads=[bkey], writes=["aoT"], tag=f"aoevac{pp}_{gt}")

        for pp in range(8):
            pair_finish(pp)

        def ln2_vec(i):
            src_ = x_own[:, i * D:(i + 1) * D]
            ub = xn2_bf[:, (i % 2) * D:((i % 2) + 1) * D]
            ubkey = f"xn2_bf{i % 2}"
            S.op("vector", lambda e: e.bn_stats(bnst[:, 0:6], src_[:, 0:512]),
                 reads=["x_own"], writes=["bnst"])
            S.op("vector", lambda e: e.bn_stats(bnst[:, 6:12], src_[:, 512:1024]),
                 reads=["x_own"], writes=["bnst"])
            S.op("vector", lambda e: e.bn_aggr(stat[:, 0:2], bnst[:]),
                 reads=["bnst"], writes=["stat"])
            S.op("vector", lambda e: e.tensor_scalar_add(stat[:, 5:6], stat[:, 1:2], 1e-5),
                 reads=["stat"], writes=["stat"])
            S.op("scalar", lambda e: e.activation(stat[:, 2:3], stat[:, 5:6], AF.Sqrt),
                 reads=["stat"], writes=["stat"])
            S.op("vector", lambda e: e.reciprocal(stat[:, 3:4], stat[:, 2:3]),
                 reads=["stat"], writes=["stat"])
            S.op("vector", lambda e: e.tensor_scalar_mul(stat[:, 4:5], stat[:, 0:1], -1.0),
                 reads=["stat"], writes=["stat"])
            S.op("vector", lambda e: e.tensor_mul(stat[:, 6:7], stat[:, 4:5], stat[:, 3:4]),
                 reads=["stat"], writes=["stat"])
            S.op("scalar", lambda e: e.mul(xn2_tmp[:], src_, stat[:, 3:4]),
                 reads=["x_own", "stat"], writes=["xn2_tmp"])
            S.op("scalar", lambda e: e.add(ub, xn2_tmp[:], stat[:, 6:7]),
                 reads=["xn2_tmp", "stat"], writes=[ubkey], tag=f"ln2_{i}")

        def ln2_tensor(i):
            ub = xn2_bf[:, (i % 2) * D:((i % 2) + 1) * D]
            ubkey = f"xn2_bf{i % 2}"
            for j in range(NJ):
                S.op("tensor",
                     lambda e, j=j, ub=ub: e.transpose(
                         ps_tr[:, j * P:(j + 1) * P], ub[:, j * P:(j + 1) * P], idbf[:]),
                     reads=[ubkey, "idbf"], writes=["ps_tr"], tag=f"tr2_{i}")
            dst = xn2T[:, i * P: i * P + 1]
            dst3 = bass.AP(xn2T, dst.offset, [dst.ap[0], [2048, NJ], [1, P]])
            src3 = ps_tr[:, 0:NJ * P].rearrange("p (j t) -> p j t", j=NJ)
            S.op("scalar", lambda e, dst3=dst3, src3=src3: e.copy(dst3, src3),
                 reads=["ps_tr"], writes=["xn2T"], tag=f"xn2Tevac{i}")
            for j in range(NJ):
                S.op("tensor",
                     lambda e, i=i, j=j: e.matmul(
                         ps_R[0:P, 128:136],
                         xn2T[:, j * 2048 + i * P: j * 2048 + (i + 1) * P],
                         gate_sb[:, j * 8:(j + 1) * 8],
                         start=(j == 0), stop=(j == NJ - 1)),
                     reads=["xn2T", "gate_sb"], writes=["ps_R"], tag=f"gate{i}")
            S.op("scalar",
                 lambda e, i=i: e.activation(sc_sb[:, i * 8:(i + 1) * 8],
                                             ps_R[0:P, 128:136], AF.Sigmoid),
                 reads=["ps_R"], writes=["sc_sb"], tag=f"sig{i}")

        for i in range(OWN_NT):
            for half in range(2):
                bank, bkey = [(ps_A, "ps_A"), (ps_B, "ps_B"), (ps_C, "ps_C")][(i * 2 + half) % 3]
                for j in range(NJ):
                    S.op("tensor",
                         lambda e, i=i, half=half, j=j, bank=bank: e.matmul(
                             bank[:],
                             aoT[:, j * 2048 + (i // 4) * GT + (i % 4) * P:
                                 j * 2048 + (i // 4) * GT + (i % 4 + 1) * P],
                             wout_sb[:, j * D + half * 512: j * D + (half + 1) * 512],
                             start=(j == 0), stop=(j == NJ - 1)),
                         reads=["aoT", "wout_sb"], writes=[bkey], tag=f"wo{i}_{half}")
                S.op("vector",
                     lambda e, i=i, half=half, bank=bank: e.tensor_add(
                         x_own[:, i * D + half * 512: i * D + (half + 1) * 512],
                         x_own[:, i * D + half * 512: i * D + (half + 1) * 512],
                         bank[:]),
                     reads=[bkey, "x_own"], writes=["x_own"], tag=f"resid{i}_{half}")
            if i >= 2:
                ln2_tensor(i - 2)
            ln2_vec(i)
        ln2_tensor(OWN_NT - 2)
        ln2_tensor(OWN_NT - 1)

        if debug:
            S.dma("gpsimd", "d_st", lambda e: e.dma_start(out=dbg["d_vT0"], in_=vT[:, 0:2048]),
                  reads=["vT"], tag="dump_vT0")
            S.dma("gpsimd", "d_st", lambda e: e.dma_start(out=dbg["d_ao0"], in_=aoT[:, 0:2048]),
                  reads=["aoT"], tag="dump_ao0")
            S.dma("gpsimd", "d_st", lambda e: e.dma_start(out=dbg["d_x1"], in_=x_own[:, 0:D]),
                  reads=["x_own"], tag="dump_x1")
        esAB.close()
        esB.close()
        S.barrier_all()

        if stage == "B":
            S.dma("sync", "d_st",
                  lambda e: e.dma_start(
                      out=y_d.rearrange("(i p) d -> p i d", p=P),
                      in_=x_own[:].rearrange("p (i d) -> p i d", d=D)),
                  reads=["x_own"], tag="outB")
            S.wait_manual("sync", "d_st", S.cnt["d_st"])
            return _emit(nc, S)

        with contextlib.ExitStack() as esC:
            wup_sb = esC.enter_context(nc.sbuf_tensor("wup_sb", [P, NJ * HID], BF16))
            wdn_sb = esC.enter_context(nc.sbuf_tensor("wdn_sb", [P, NM * D], BF16))
            h_g = esC.enter_context(nc.sbuf_tensor("h_g", [P, 2 * NM * GT], BF16))

            for i in range(OWN_NT):
                sl = slice(i * 8, (i + 1) * 8)
                S.op("vector",
                     lambda e, sl=sl: e.tensor_add(s2_sb[:, sl], sc_sb[:, sl], bias_b[:]),
                     reads=["sc_sb", "bias_b"], writes=["s2_sb"], tag=f"bias{i}")
                S.op("vector",
                     lambda e, sl=sl: e.reduce_max(stat[:, 16:17], s2_sb[:, sl], axis=AX.X),
                     reads=["s2_sb"], writes=["stat"])
                S.op("vector",
                     lambda e, sl=sl: e.tensor_scalar(stat[:, 18:26], s2_sb[:, sl],
                                                      stat[:, 16:17], -1e30,
                                                      ALU.is_ge, ALU.mult),
                     reads=["s2_sb", "stat"], writes=["stat"], tag=f"eqm{i}")
                S.op("vector",
                     lambda e, sl=sl: e.tensor_add(stat[:, 18:26], stat[:, 18:26],
                                                   s2_sb[:, sl]),
                     reads=["s2_sb", "stat"], writes=["stat"])
                S.op("vector",
                     lambda e: e.reduce_max(stat[:, 17:18], stat[:, 18:26], axis=AX.X),
                     reads=["stat"], writes=["stat"])
                S.op("vector",
                     lambda e, sl=sl: e.tensor_scalar(stat[:, 18:26], s2_sb[:, sl],
                                                      stat[:, 17:18], None, ALU.is_ge),
                     reads=["s2_sb", "stat"], writes=["stat"])
                S.op("vector",
                     lambda e, sl=sl: e.tensor_mul(stat[:, 18:26], stat[:, 18:26],
                                                   sc_sb[:, sl]),
                     reads=["sc_sb", "stat"], writes=["stat"])
                S.op("vector",
                     lambda e, sl=sl: e.tensor_mul(s2_sb[:, sl], stat[:, 18:26],
                                                   stat[:, 18:26]),
                     reads=["stat"], writes=["s2_sb"], tag=f"s2_{i}")

            if debug:
                S.dma("gpsimd", "d_st", lambda e: e.dma_start(out=dbg["d_xn0"], in_=xn2T[:, 0:2048]),
                      reads=["xn2T"], tag="dump_xn0")
                S.dma("gpsimd", "d_st", lambda e: e.dma_start(out=dbg["d_sc"], in_=sc_sb[:]),
                      reads=["sc_sb"], tag="dump_sc")
                S.dma("gpsimd", "d_st", lambda e: e.dma_start(out=dbg["d_s2"], in_=s2_sb[:]),
                      reads=["s2_sb"], tag="dump_s2")
            for gx in range(NEX):
                for q2 in range(2):
                    S.dma("gpsimd", "d_wu",
                          lambda e, gx=gx, q2=q2: e.dma_start(
                              out=wup_sb[:, q2 * 4 * HID:(q2 + 1) * 4 * HID]
                                  .rearrange("p (j f) -> p j f", j=4),
                              in_=wup_d[gx][q2 * 512:(q2 + 1) * 512, :]
                                  .rearrange("(j p) f -> p j f", p=P)),
                          writes=["wup_sb"], tag=f"wup{gx}_{q2}")
                S.stamp_write("wup_sb", "d_wu", S.cnt["d_wu"])
                for q2, (m0, mn) in enumerate(((0, 5), (5, 6))):
                    S.dma("gpsimd", "d_wd",
                          lambda e, gx=gx, m0=m0, mn=mn: e.dma_start(
                              out=wdn_sb[:, m0 * D:(m0 + mn) * D]
                                  .rearrange("p (m c) -> p m c", m=mn),
                              in_=wdn_d[gx][m0 * P:(m0 + mn) * P, :]
                                  .rearrange("(m p) c -> p m c", p=P)),
                          writes=["wdn_sb"], tag=f"wdn{gx}_{q2}")
                S.stamp_write("wdn_sb", "d_wd", S.cnt["d_wd"])
                for grp in range(OWN_NG):
                    hb = ((gx * OWN_NG + grp) % 2) * NM * GT
                    hpar = (gx * OWN_NG + grp) % 2
                    for m in range(NM):
                        bank, bkey = [(ps_A, "ps_A"), (ps_B, "ps_B"), (ps_C, "ps_C")][m % 3]
                        for j in range(NJ):
                            S.op("tensor",
                                 lambda e, m=m, j=j, grp=grp, bank=bank: e.matmul(
                                     bank[:],
                                     wup_sb[:, j * HID + m * P: j * HID + (m + 1) * P],
                                     xn2T[:, j * 2048 + grp * GT: j * 2048 + (grp + 1) * GT],
                                     start=(j == 0), stop=(j == NJ - 1)),
                                 reads=["wup_sb", "xn2T"], writes=[bkey],
                                 tag=f"up{gx}_{grp}_{m}")
                        S.op("scalar",
                             lambda e, m=m, bank=bank, hb=hb: e.activation(
                                 h_g[:, hb + m * GT: hb + (m + 1) * GT], bank[:], AF.Relu),
                             reads=[bkey], writes=[f"h_g{hpar}_{m}"],
                             tag=f"relu{gx}_{grp}_{m}")
                        S.op("vector",
                             lambda e, m=m, hb=hb: e.tensor_mul(
                                 h_g[:, hb + m * GT: hb + (m + 1) * GT],
                                 h_g[:, hb + m * GT: hb + (m + 1) * GT],
                                 h_g[:, hb + m * GT: hb + (m + 1) * GT]),
                             reads=[f"h_g{hpar}_{m}"], writes=[f"h_g{hpar}_{m}"],
                             tag=f"sqr{gx}_{grp}_{m}")
                        if debug and gx == 0 and grp == 0 and m == 0:
                            S.dma("gpsimd", "d_st",
                                  lambda e: e.dma_start(out=dbg["d_h0"],
                                                        in_=h_g[:, 0:GT]),
                                  reads=["h_g0"], tag="dump_h0")
                    if gx == NEX - 1 and grp > 0:
                        q4 = grp - 1
                        S.dma("sync", "d_st",
                              lambda e, q4=q4: e.dma_start(
                                  out=y_d[q4 * 512:(q4 + 1) * 512, :]
                                      .rearrange("(i p) d -> p i d", p=P),
                                  in_=x_own[:, q4 * 4 * D:(q4 + 1) * 4 * D]
                                      .rearrange("p (i d) -> p i d", d=D)),
                              reads=[f"x_fin{q4}"], tag=f"out{q4}")
                    for s in range(4):
                        i = grp * 4 + s
                        for half in range(2):
                            bank, bkey = (ps_Y, "ps_Y") if (s * 2 + half) % 2 == 0 else (ps_Z, "ps_Z")
                            for m in range(NM):
                                S.op("tensor",
                                     lambda e, m=m, s=s, half=half, bank=bank, hb=hb: e.matmul(
                                         bank[:],
                                         h_g[:, hb + m * GT + s * P: hb + m * GT + (s + 1) * P],
                                         wdn_sb[:, m * D + half * 512:
                                                m * D + (half + 1) * 512],
                                         start=(m == 0), stop=(m == NM - 1)),
                                     reads=[f"h_g{hpar}_{m}", "wdn_sb"], writes=[bkey],
                                     tag=f"dn{gx}_{grp}_{s}_{half}")
                            xsl = x_own[:, i * D + half * 512: i * D + (half + 1) * 512]
                            xkeys = ["x_own"] if gx < 8 else ["x_own", f"x_fin{grp}"]
                            if gx < 8:
                                S.op("vector",
                                     lambda e, xsl=xsl, i=i, gx=gx, bank=bank:
                                     e.scalar_tensor_tensor(
                                         xsl, bank[:],
                                         s2_sb[:, i * 8 + gx: i * 8 + gx + 1], xsl,
                                         ALU.mult, ALU.add),
                                     reads=[bkey, "s2_sb", "x_own"], writes=xkeys,
                                     tag=f"acc{gx}_{i}_{half}")
                            else:
                                S.op("vector",
                                     lambda e, xsl=xsl, bank=bank: e.tensor_add(
                                         xsl, xsl, bank[:]),
                                     reads=[bkey, "x_own"], writes=xkeys,
                                     tag=f"accS_{i}_{half}")

        if debug:
            S.dma("gpsimd", "d_st", lambda e: e.dma_start(out=dbg["d_xg0"],
                                                          in_=x_own[:, 0:D]),
                  reads=["x_own"], tag="dump_xg0")
        esCpre.close()
        S.dma("sync", "d_st",
              lambda e: e.dma_start(
                  out=y_d[3 * 512:4 * 512, :].rearrange("(i p) d -> p i d", p=P),
                  in_=x_own[:, 3 * 4 * D:4 * 4 * D].rearrange("p (i d) -> p i d", d=D)),
              reads=["x_fin3"], tag="out3")
        S.wait_manual("sync", "d_st", S.cnt["d_st"])

        return _emit(nc, S)


def _emit(nc, S):
    sems = S.sems
    with nc.Block() as block:
        def emit(eng_name):
            def body(e):
                for rec in S.ops[eng_name]:
                    for smn, v in rec["waits"]:
                        e.wait_ge(sems[smn], v)
                    inst = rec["fn"](e) if rec["fn"] is not None else None
                    if rec["incs"]:
                        if inst is None:
                            inst = e.nop()
                        for smn, d in rec["incs"]:
                            inst.then_inc(sems[smn], d)
            return body
        block.sync(emit("sync"))
        block.gpsimd(emit("gpsimd"))
        block.tensor(emit("tensor"))
        block.vector(emit("vector"))
        block.scalar(emit("scalar"))
    return nc


_NC = None


def _get_nc():
    global _NC
    if _NC is None:
        _NC = build_nc()
    return _NC


def kernel(L0_attn_x, L0_attn_gamma, L0_attn_w_qkv, L0_attn_temperature, L0_attn_w_out,
           L0_ffn_gate_w, L0_ffn_expert_bias, L0_ffn_w_up, L0_ffn_w_down,
           L0_ffn_shared_w_up, L0_ffn_shared_w_down):
    x = np.ascontiguousarray(np.asarray(L0_attn_x, dtype=np.float32))
    gamma = np.asarray(L0_attn_gamma, dtype=np.float32).reshape(D)
    w_qkv = np.ascontiguousarray(np.asarray(L0_attn_w_qkv, dtype=np.float32))
    temp = np.asarray(L0_attn_temperature, dtype=np.float32).reshape(1, NH)
    w_out = np.ascontiguousarray(np.asarray(L0_attn_w_out, dtype=np.float32))
    gate_w = np.asarray(L0_ffn_gate_w, dtype=np.float32)
    ebias = np.asarray(L0_ffn_expert_bias, dtype=np.float32).reshape(1, 8)
    w_up = np.asarray(L0_ffn_w_up, dtype=np.float32)
    w_dn = np.asarray(L0_ffn_w_down, dtype=np.float32)
    swup = np.asarray(L0_ffn_shared_w_up, dtype=np.float32)
    swdn = np.asarray(L0_ffn_shared_w_down, dtype=np.float32)

    gam_t = np.ascontiguousarray(gamma.reshape(8, P).T)
    gate_wT = np.ascontiguousarray(gate_w.T)
    wupT = np.ascontiguousarray(
        np.concatenate([w_up.transpose(0, 2, 1), swup.T[None]], axis=0))
    wdnT = np.ascontiguousarray(
        np.concatenate([w_dn.transpose(0, 2, 1), swdn.T[None]], axis=0))
    idbf = np.eye(P, dtype=np.float32)

    in_maps = []
    own_slices = []
    for c in range(8):
        b = c // 4
        ownoff = (c % 4) * 2048
        xb = x[b]
        rest = np.r_[0:ownoff, ownoff + 2048:TOK]
        slab = np.ascontiguousarray(
            np.concatenate([xb[ownoff:ownoff + 2048], xb[rest]], axis=0))
        own_slices.append((b, ownoff))
        in_maps.append({
            "xs": slab,
            "wqkv": w_qkv,
            "gamt": gam_t,
            "temp": temp,
            "wout": w_out,
            "gatew": gate_wT,
            "ebias": ebias,
            "wup": wupT,
            "wdn": wdnT,
            "idbf": idbf,
        })

    nc = _get_nc()
    res = run_bass_kernel_spmd(nc, in_maps, core_ids=list(range(8)))

    out = np.empty((2, TOK, D), dtype=np.float32)
    for c in range(8):
        b, ownoff = own_slices[c]
        out[b, ownoff:ownoff + 2048] = res.results[c]["y"]
    return out
